# version 23
# baseline (speedup 1.0000x reference)
import contextlib
import numpy as np
import concourse.bass as bass
import concourse.mybir as mybir
from concourse.bass_utils import run_bass_kernel_spmd

F32 = mybir.dt.float32
AF = mybir.ActivationFunctionType
ALU = mybir.AluOpType
AX = mybir.AxisListType

D = 1024
DQ = 64
FF = 4096
ND = D // 128
NF = FF // 128
EPS = 1e-5
NEG = -1.0e30
QSCALE = 0.125


class Buf:
    __slots__ = ("writer", "readers", "name")

    def __init__(self, name=""):
        self.writer = None
        self.readers = []
        self.name = name


class Prog:
    ENGINES = ("tensor", "vector", "scalar", "sync", "gpsimd")
    ESEM = {"tensor": "pe", "vector": "ve", "scalar": "se"}
    DPOOL = {"sync": [f"ld{i}" for i in range(12)],
             "gpsimd": [f"st{i}" for i in range(4)],
             "scalar": [f"sd{i}" for i in range(6)]}

    def __init__(self):
        self.sched = {e: [] for e in self.ENGINES}
        self.ctr = {s: 0 for s in ("pe", "ve", "se")}
        for pool in self.DPOOL.values():
            for s in pool:
                self.ctr[s] = 0
        self.dma_n = {e: 0 for e in self.DPOOL}
        self.wm = {e: {} for e in self.ENGINES}

    def _waits(self, eng, tokens):
        merged = {}
        for t in tokens:
            if t is None:
                continue
            s, v = t
            if eng == "tensor" and s == "pe":
                continue
            if merged.get(s, -1) < v:
                merged[s] = v
        out = []
        for s, v in merged.items():
            if self.wm[eng].get(s, 0) >= v:
                continue
            out.append((s, v))
            self.wm[eng][s] = v
        return out

    def emit(self, eng, build, reads=(), writes=(), dma=False, extra=()):
        toks = []
        for b in reads:
            toks.append(b.writer)
        for b in writes:
            toks.append(b.writer)
            toks.extend(b.readers)
        toks.extend(extra)

        if dma:
            pool = self.DPOOL[eng]
            sem = pool[self.dma_n[eng] % len(pool)]
            self.dma_n[eng] += 1
            if self.ctr[sem] > 0:
                toks.append((sem, self.ctr[sem]))
            amt = 16
        else:
            sem, amt = self.ESEM[eng], 1
        waits = self._waits(eng, toks)
        self.ctr[sem] += amt
        token = (sem, self.ctr[sem])

        def run(engobj, sems, _waits=waits, _build=build, _sem=sem, _amt=amt):
            for s, v in _waits:
                engobj.wait_ge(sems[s], v)
            ins = _build(engobj, sems)
            ins.then_inc(sems[_sem], _amt)

        self.sched[eng].append(run)

        for b in reads:
            b.readers.append(token)
        for b in writes:
            b.writer = token
            b.readers = []
        return token

    def barrier(self):
        snap = dict(self.ctr)
        for eng in self.ENGINES:
            waits = []
            for s, v in snap.items():
                if v > 0 and self.wm[eng].get(s, 0) < v:
                    waits.append((s, v))
                    self.wm[eng][s] = v
            if not waits:
                continue

            def run(engobj, sems, _waits=waits):
                for s, v in _waits:
                    engobj.wait_ge(sems[s], v)

            self.sched[eng].append(run)


def build_program(T):
    NT = T // 128
    NPAN = T // 256
    NSUP = NT // 2

    nc = bass.Bass("TRN2", target_bir_lowering=False)

    x_h = nc.dram_tensor("x", [T, D], F32, kind="ExternalInput").ap()
    wq_h = nc.dram_tensor("wq", [D, DQ], F32, kind="ExternalInput").ap()
    wk_h = nc.dram_tensor("wk", [D, DQ], F32, kind="ExternalInput").ap()
    wv_h = nc.dram_tensor("wv", [D, D], F32, kind="ExternalInput").ap()
    wfc_h = nc.dram_tensor("wfc", [FF, D], F32, kind="ExternalInput").ap()
    bfc_h = nc.dram_tensor("bfc", [FF], F32, kind="ExternalInput").ap()
    afc_h = nc.dram_tensor("afc", [16, D], F32, kind="ExternalInput").ap()
    bfcB_h = nc.dram_tensor("bfcB", [FF, 16], F32, kind="ExternalInput").ap()
    wpr_h = nc.dram_tensor("wpr", [D, FF], F32, kind="ExternalInput").ap()
    bpr_h = nc.dram_tensor("bpr", [D], F32, kind="ExternalInput").ap()
    apr_h = nc.dram_tensor("apr", [16, FF], F32, kind="ExternalInput").ap()
    bprB_h = nc.dram_tensor("bprB", [D, 16], F32, kind="ExternalInput").ap()
    ident_h = nc.dram_tensor("ident", [128, 128], F32, kind="ExternalInput").ap()
    cmask_h = nc.dram_tensor("cmask", [128, 128], F32, kind="ExternalInput").ap()

    y_h = nc.dram_tensor("y", [T, D], F32, kind="ExternalOutput").ap()
    y2_h = nc.dram_tensor("y2", [T, D], F32, kind="Internal").ap()
    xa_h = nc.dram_tensor("xa", [T, D], F32, kind="Internal").ap()
    hT_h = nc.dram_tensor("hT", [FF, T], F32, kind="Internal").ap()
    w1t_h = nc.dram_tensor("w1t", [NF * 128, D], F32, kind="Internal").ap()
    w2t_h = nc.dram_tensor("w2t", [NF * 128, D], F32, kind="Internal").ap()
    yt_h = nc.dram_tensor("yt", [T // 256 * 128, 2048], F32, kind="Internal").ap()

    es = contextlib.ExitStack()
    sb = lambda name, shape: es.enter_context(nc.sbuf_tensor(name, shape, F32))

    A1 = sb("A1", [128, 32768])
    AW = sb("AW", [128, 8192])
    A2 = sb("A2", [128, 5248])
    A4 = sb("A4", [128, 2048])
    A5 = sb("A5", [128, 3072])
    A7 = sb("A7", [128, 512])
    ident = sb("identt", [128, 128])
    swq = sb("swq", [128, 8 * DQ])
    swk = sb("swk", [128, 8 * DQ])
    sbfc = sb("sbfc", [128, 32])
    sones = sb("sones", [1, 128])
    ST = sb("stats", [128, 32])

    c_sum, c_sq, c_m, c_var, c_r, c_nmr = 0, 1, 2, 3, 4, 5
    c_max, c_tm, c_es, c_rs, c_nm8 = 6, 7, 8, 9, 10
    c_wmax, c_ws1, c_ws2, c_tmp = 11, 12, 13, 14
    c_eps, c_q8 = 15, 16
    c_one, c_cg, c_wsi = 17, 18, 19
    c_ptm, c_ptmp = 20, 21
    d_sum, d_sq, d_m, d_var, d_r, d_nmr, d_t1, d_t2 = 24, 25, 26, 27, 28, 29, 30, 31
    C32 = 12582912.0
    C64 = 6755399441055744.0
    E30 = 2.0 ** -30

    def stc(i):
        return ST[:, i:i + 1]

    psum = [es.enter_context(nc.psum_tensor(f"ps{i}", [128, 512], F32))
            for i in range(8)]

    P = Prog()

    bA1 = [Buf(f"A1_{i}") for i in range(64)]
    bAW = [Buf(f"AW_{i}") for i in range(16)]
    bA2s = [Buf(f"A2P_{i}") for i in range(8)]
    bmask = Buf("mask")
    bbprT = Buf("bprT")
    bA4 = [Buf(f"A4_{i}") for i in range(4)]
    bA5 = [Buf("A5_0"), Buf("A5_1"), Buf("A5_2")]
    bA7 = [Buf(f"A7_{i}") for i in range(4)]
    bPS = [Buf(f"ps{i}") for i in range(8)]
    bident = Buf("ident")
    bswq, bswk = Buf("swq"), Buf("swk")
    bsbfc = Buf("sbfc")
    bsones = Buf("sones")
    bST = [Buf(f"st{i}") for i in range(32)]
    bHBM = {k: Buf(k) for k in ("xa", "hT", "w1t", "w2t", "y")}
    bhT = [Buf(f"hT{i}") for i in range(NF)]
    byt = [Buf(f"yt{i}") for i in range(T // 256)]
    by2 = [Buf(f"y2_{i}") for i in range(NT)]

    def dma(eng, dst, src, reads, writes, extra=()):
        return P.emit(eng, lambda e, s, _d=dst, _s=src: e.dma_start(out=_d, in_=_s),
                      reads=reads, writes=writes, dma=True, extra=extra)

    def mm(out, lhsT, rhs, start, stop, reads, writes, extra=(), is_tr=None):
        return P.emit("tensor",
                      lambda e, s, _o=out, _l=lhsT, _r=rhs, _st=start, _sp=stop,
                      _tr=is_tr:
                      e.matmul(_o, _l, _r, start=_st, stop=_sp, is_transpose=_tr),
                      reads=reads, writes=writes, extra=extra)

    def tr(out, in_, idn, reads, writes, extra=()):
        return P.emit("tensor",
                      lambda e, s, _o=out, _i=in_, _d=idn: e.transpose(_o, _i, _d),
                      reads=reads, writes=writes, extra=extra)

    def vop(fn, reads, writes, extra=()):
        return P.emit("vector", fn, reads=reads, writes=writes, extra=extra)

    def sop(fn, reads, writes, extra=()):
        return P.emit("scalar", fn, reads=reads, writes=writes, extra=extra)

    dma("sync", ident[:], ident_h, [], [bident])
    swq_src = bass.AP(wq_h.tensor, 0, [[DQ, 128], [128 * DQ, ND], [1, DQ]])
    swk_src = bass.AP(wk_h.tensor, 0, [[DQ, 128], [128 * DQ, ND], [1, DQ]])
    dma("sync", swq[:], swq_src, [], [bswq])
    dma("sync", swk[:], swk_src, [], [bswk])
    P.emit("vector", lambda e, s: e.memset(sones[:], 1.0), [], [bsones])
    P.emit("vector", lambda e, s: e.memset(ST[:, 15:16], EPS), [], [bST[15]])
    P.emit("vector", lambda e, s: e.memset(ST[:, 16:17], QSCALE), [], [bST[16]])
    P.emit("vector", lambda e, s: e.memset(stc(c_one), 1.0), [], [bST[c_one]])
    P.emit("vector", lambda e, s: e.memset(stc(c_tmp), E30), [], [bST[c_tmp]])
    P.emit("vector", lambda e, s: e.tensor_add(stc(c_tmp), stc(c_tmp), stc(c_one)),
           [bST[c_one], bST[c_tmp]], [bST[c_tmp]])
    P.emit("vector", lambda e, s: e.tensor_sub(stc(c_tmp), stc(c_tmp), stc(c_one)),
           [bST[c_one], bST[c_tmp]], [bST[c_tmp]])
    P.emit("vector", lambda e, s: e.tensor_scalar_mul(stc(c_tmp), stc(c_tmp), 2.0 ** 30),
           [bST[c_tmp]], [bST[c_tmp]])
    P.emit("vector", lambda e, s: e.memset(stc(c_cg), C32), [], [bST[c_cg]])
    P.emit("vector", lambda e, s: e.scalar_tensor_tensor(
        stc(c_cg), stc(c_tmp), C64, stc(c_cg), ALU.mult, ALU.add),
        [bST[c_tmp], bST[c_cg]], [bST[c_cg]])

    bfc_raw = A4[0:32, 640:768]
    dma("sync", bfc_raw, bass.AP(bfc_h.tensor, 0, [[128, 32], [1, 128]]),
        [], [bA4[1]])
    tr(psum[4][0:128, 0:32], bfc_raw, ident[0:32, 0:32], [bA4[1], bident], [bPS[4]])
    vop(lambda e, s: e.tensor_copy(sbfc[:], psum[4][0:128, 0:32]), [bPS[4]], [bsbfc])

    pL = [A2[:, 0:1024], A2[:, 1024:2048]]
    bpL = [Buf("pL0"), Buf("pL1")]
    pqa = A2[:, 2048:3072]
    bpqa = Buf("pqa")
    pstage = A2[:, 3072:4096]
    bpstage = Buf("pstage")
    aCh = A2[0:16, 4096:5120]
    baCh = Buf("aCh")
    bTs = [A7[0:16, 0:128], A7[0:16, 128:256]]
    bbT = [Buf("bT0"), Buf("bT1")]
    praw = [A7[:, 256:272], A7[:, 272:288]]
    bpraw = [Buf("praw0"), Buf("praw1")]

    def gen_prep():
        for wname, w_h, b_h, a_h, n_tiles, tile_cols, c_ws in (
                ("w1", wfc_h, bfcB_h, afc_h, NF, D, c_ws1),
                ("w2", wpr_h, bprB_h, apr_h, ND, FF, c_ws2)):
            nq = tile_cols // 1024
            for u in range(n_tiles * nq):
                t, q = u // nq, u % nq
                sl = pL[u % 2]
                dma("scalar", sl, w_h[128 * t:128 * t + 128,
                                      1024 * q:1024 * q + 1024], [], [bpL[u % 2]])
                if u == 0:
                    vop(lambda e, s, _sl=sl: e.tensor_reduce(
                        stc(c_wmax), _sl, AX.X, ALU.max, apply_absolute_value=True),
                        [bpL[u % 2]], [bST[c_wmax]])
                else:
                    vop(lambda e, s, _sl=sl: e.tensor_reduce(
                        stc(c_ptm), _sl, AX.X, ALU.max, apply_absolute_value=True),
                        [bpL[u % 2]], [bST[c_ptm]])
                    vop(lambda e, s: e.tensor_max(stc(c_wmax), stc(c_wmax),
                                                  stc(c_ptm)),
                        [bST[c_ptm], bST[c_wmax]], [bST[c_wmax]])
                yield
            tr(psum[4][0:1, 0:128], stc(c_wmax), ident[:],
               [bST[c_wmax], bident], [bPS[4]])
            vop(lambda e, s: e.reduce_max(ST[0:1, c_ptmp:c_ptmp + 1],
                                          psum[4][0:1, 0:128], AX.X),
                [bPS[4]], [bST[c_ptmp]])
            vop(lambda e, s: e.tensor_scalar_mul(
                ST[0:1, c_ptmp:c_ptmp + 1], ST[0:1, c_ptmp:c_ptmp + 1],
                1.0 / 127.0), [bST[c_ptmp]], [bST[c_ptmp]])
            mm(psum[4][:, 0:1], sones[:], ST[0:1, c_ptmp:c_ptmp + 1], True, True,
               [bsones, bST[c_ptmp]], [bPS[4]])
            vop(lambda e, s, _c=c_ws: e.tensor_copy(stc(_c), psum[4][:, 0:1]),
                [bPS[4]], [bST[c_ws]])
            vop(lambda e, s, _c=c_ws: e.reciprocal(stc(c_wsi), stc(_c)),
                [bST[c_ws]], [bST[c_wsi]])
            yield
            for q in range(nq):
                if wname == "w1":
                    if q == 0:
                        dma("scalar", aCh, a_h, [], [baCh])
                else:
                    dma("scalar", aCh, a_h[:, 1024 * q:1024 * q + 1024], [], [baCh])
                for t in range(n_tiles):
                    u = q * n_tiles + t
                    rs = praw[u % 2]
                    dma("scalar", rs, b_h[128 * t:128 * t + 128, :], [], [bpraw[u % 2]])
                    tr(psum[4][0:16, 0:128], rs, ident[:],
                       [bpraw[u % 2], bident], [bPS[4]])
                    vop(lambda e, s, _u=u: e.tensor_scalar_mul(
                        bTs[_u % 2], psum[4][0:16, 0:128], 2.0),
                        [bPS[4]], [bbT[u % 2]])
                    sl = pL[u % 2]
                    dma("scalar", sl, w_h[128 * t:128 * t + 128,
                                          1024 * q:1024 * q + 1024], [], [bpL[u % 2]])
                    vop(lambda e, s, _sl=sl: e.tensor_single_scalar(
                        pqa, _sl, stc(c_wsi), ALU.mult),
                        [bpL[u % 2], bST[c_wsi]], [bpqa])
                    vop(lambda e, s: e.tensor_scalar(
                        pqa, pqa, stc(c_cg), stc(c_cg), ALU.add, ALU.subtract),
                        [bpqa, bST[c_cg]], [bpqa])
                    vop(lambda e, s, _c=c_ws: e.tensor_single_scalar(
                        pqa, pqa, stc(_c), ALU.mult),
                        [bpqa, bST[c_ws]], [bpqa])
                    for cc in range(8):
                        tb = 5 + (cc % 2)
                        mm(psum[tb][:, 0:128], pqa[:, 128 * cc:128 * cc + 128],
                           ident[:], True, False,
                           [bpqa, bident], [bPS[tb]], is_tr=True)
                        mm(psum[tb][:, 0:128], aCh[:, 128 * cc:128 * cc + 128],
                           bTs[u % 2], False, True,
                           [baCh, bbT[u % 2]], [bPS[tb]])
                        sop(lambda e, s, _cc=cc, _tb=tb: e.activation(
                            pstage[:, 128 * _cc:128 * _cc + 128],
                            psum[_tb][:, 0:128], AF.Identity),
                            [bPS[tb]], [bpstage])
                    if wname == "w1":
                        dma("gpsimd", w1t_h[128 * t:128 * t + 128, :],
                            pstage[:], [bpstage], [bHBM["w1t"]])
                    else:
                        dst = bass.AP(w2t_h.tensor, 128 * t + 1024 * q * D,
                                      [[D, 128], [128 * D, 8], [1, 128]])
                        dma("gpsimd", dst, pstage[:], [bpstage], [bHBM["w2t"]])
                    yield

    P.barrier()

    def emit_ln_head(xslot, bslot):
        vop(lambda e, s, _x=xslot: e.reduce_sum(stc(c_sum), _x, AX.X),
            [bslot], [bST[c_sum]])
        sop(lambda e, s, _x=xslot: e.activation(psum[7][:, 0:512], _x[:, 0:512],
                                                AF.Square, accum_out=stc(c_sq)),
            [bslot], [bPS[7], bST[c_sq]])
        sop(lambda e, s, _x=xslot: e.activation(psum[7][:, 0:512], _x[:, 512:1024],
                                                AF.Square, accum_out=stc(c_tm)),
            [bslot], [bPS[7], bST[c_tm]])

    def emit_ln_tail(xslot, bslot):
        vop(lambda e, s: e.tensor_add(stc(c_sq), stc(c_sq), stc(c_tm)),
            [bST[c_sq], bST[c_tm]], [bST[c_sq]])
        vop(lambda e, s: e.tensor_scalar_mul(stc(c_m), stc(c_sum), 1.0 / D),
            [bST[c_sum]], [bST[c_m]])
        vop(lambda e, s: e.tensor_mul(stc(c_tmp), stc(c_m), stc(c_m)),
            [bST[c_m]], [bST[c_tmp]])
        vop(lambda e, s: e.scalar_tensor_tensor(
            stc(c_var), stc(c_sq), 1.0 / D, stc(c_tmp), ALU.mult, ALU.subtract),
            [bST[c_sq], bST[c_tmp]], [bST[c_var]])
        sop(lambda e, s: e.activation(stc(c_tm), stc(c_var), AF.Sqrt,
                                      bias=stc(c_eps)),
            [bST[c_var], bST[c_eps]], [bST[c_tm]])
        vop(lambda e, s: e.reciprocal(stc(c_r), stc(c_tm)),
            [bST[c_tm]], [bST[c_r]])
        vop(lambda e, s: e.scalar_tensor_tensor(
            stc(c_nmr), stc(c_m), -1.0, stc(c_r), ALU.mult, ALU.mult),
            [bST[c_m], bST[c_r]], [bST[c_nmr]])
        sop(lambda e, s, _x=xslot: e.activation(_x, _x, AF.Identity,
                                                bias=stc(c_nmr), scale=stc(c_r)),
            [bST[c_nmr], bST[c_r], bslot], [bslot])

    def emit_ln(xslot, bslot):
        emit_ln_head(xslot, bslot)
        emit_ln_tail(xslot, bslot)

    qT = AW[0:64, 0:4096]
    kT = AW[0:64, 4096:8192]
    bqT, bkT = Buf("qT"), Buf("kT")

    def gen_p1ab():
        wv_src = bass.AP(wv_h.tensor, 0, [[D, 128], [128 * D, ND], [1, D]])
        dma("sync", AW[:], wv_src, [], list(bAW))
        def slot(t):
            return A5[:, 1024 * (t % 3):1024 * (t % 3) + 1024]

        def ld1(t):
            dma("sync", slot(t), x_h[128 * t:128 * t + 128, :], [], [bA5[t % 3]])

        def ln_post(t):
            emit_ln_tail(slot(t), bA5[t % 3])
            dma("gpsimd", xa_h[128 * t:128 * t + 128, :], slot(t), [bA5[t % 3]],
                [bHBM["xa"]])

        ld1(0)
        ld1(1)
        emit_ln_head(slot(0), bA5[0])
        ln_post(0)
        for p in range(NPAN):
            for tt in range(2):
                t = 2 * p + tt
                xs = slot(t)
                bxs = bA5[t % 3]
                if t + 2 < NT:
                    ld1(t + 2)
                if t + 1 < NT:
                    emit_ln_head(slot(t + 1), bA5[(t + 1) % 3])
                for c in range(ND):
                    tb = c % 2
                    tr(psum[tb][:, 0:128], xs[:, 128 * c:128 * c + 128], ident[:],
                       [bxs, bident], [bPS[tb]])
                    vop(lambda e, s, _c=c, _tt=tt, _tb=tb: e.tensor_copy(
                        A4[:, 256 * _c + 128 * _tt:256 * _c + 128 * _tt + 128],
                        psum[_tb][:, 0:128]),
                        [bPS[tb]], [bA4[(256 * c + 128 * tt) // 512]])
                for hh in range(2):
                    vb = 2 + hh
                    for c in range(ND):
                        mm(psum[vb][:, 0:512],
                           A4[:, 256 * c + 128 * tt:256 * c + 128 * tt + 128],
                           AW[:, 1024 * c + 512 * hh:1024 * c + 512 * hh + 512],
                           c == 0, c == ND - 1,
                           [bA4[(256 * c) // 512],
                            bAW[(1024 * c + 512 * hh) // 512]],
                           [bPS[vb]])
                    vop(lambda e, s, _t=t, _hh=hh, _vb=vb: e.tensor_copy(
                        A1[:, 1024 * _t + 512 * _hh:1024 * _t + 512 * _hh + 512],
                        psum[_vb][:, 0:512]),
                        [bPS[vb]], [bA1[(1024 * t + 512 * hh) // 512]])
                if tt == 1:
                    dma("gpsimd", yt_h[128 * p:128 * p + 128, :], A4[:],
                        [bA4[0], bA4[1], bA4[2], bA4[3]], [byt[p]])
                if t + 1 < NT:
                    ln_post(t + 1)
                yield
        pnl = [A4, A5]
        bpnl = [[bA4[0], bA4[1], bA4[2], bA4[3]], [bA5[0], bA5[1]]]
        for p in range(NPAN):
            pb_ = pnl[p % 2]
            bb_ = bpnl[p % 2]
            dma("sync", pb_[:, 0:2048], yt_h[128 * p:128 * p + 128, :],
                [byt[p]], bb_)
            for qk in range(2):
                bank = 2 + qk
                wsb, bwsb = (swq, bswq) if qk == 0 else (swk, bswk)
                dst, bdst = (qT, bqT) if qk == 0 else (kT, bkT)
                for c in range(ND):
                    mm(psum[bank][0:64, 0:256], wsb[:, DQ * c:DQ * c + DQ],
                       pb_[:, 256 * c:256 * c + 256], c == 0, c == ND - 1,
                       [bwsb] + bb_, [bPS[bank]])
                aw_lo = (4096 * qk + 256 * p) // 512
                vop(lambda e, s, _p=p, _d=dst, _b=bank: e.tensor_copy(
                    _d[:, 256 * _p:256 * _p + 256], psum[_b][0:64, 0:256]),
                    [bPS[bank]], [bdst, bAW[aw_lo]])
            yield

    _SENT = object()
    gp, g1 = gen_prep(), gen_p1ab()
    ap_, a1_ = True, True
    while ap_ or a1_:
        for _ in range(3):
            if ap_:
                ap_ = next(gp, _SENT) is not _SENT
        if a1_:
            a1_ = next(g1, _SENT) is not _SENT

    P.barrier()

    Pbuf = A2[:, 0:4096]
    mask128 = A2[:, 4096:4224]
    dma("sync", mask128, cmask_h, [], [bmask])

    def emit_ln2p(xslot, bslot):
        vop(lambda e, s, _x=xslot: e.reduce_sum(stc(d_sum), _x, AX.X),
            [bslot], [bST[d_sum]])
        sop(lambda e, s, _x=xslot: e.activation(psum[6][:, 0:512], _x[:, 0:512],
                                                AF.Square, accum_out=stc(d_sq)),
            [bslot], [bPS[6], bST[d_sq]])
        sop(lambda e, s, _x=xslot: e.activation(psum[6][:, 0:512], _x[:, 512:1024],
                                                AF.Square, accum_out=stc(d_t1)),
            [bslot], [bPS[6], bST[d_t1]])
        vop(lambda e, s: e.tensor_add(stc(d_sq), stc(d_sq), stc(d_t1)),
            [bST[d_sq], bST[d_t1]], [bST[d_sq]])
        vop(lambda e, s: e.tensor_scalar_mul(stc(d_m), stc(d_sum), 1.0 / D),
            [bST[d_sum]], [bST[d_m]])
        vop(lambda e, s: e.tensor_mul(stc(d_t2), stc(d_m), stc(d_m)),
            [bST[d_m]], [bST[d_t2]])
        vop(lambda e, s: e.scalar_tensor_tensor(
            stc(d_var), stc(d_sq), 1.0 / D, stc(d_t2), ALU.mult, ALU.subtract),
            [bST[d_sq], bST[d_t2]], [bST[d_var]])
        sop(lambda e, s: e.activation(stc(d_t1), stc(d_var), AF.Ln,
                                      bias=stc(c_eps)),
            [bST[d_var], bST[c_eps]], [bST[d_t1]])
        sop(lambda e, s: e.activation(stc(d_r), stc(d_t1), AF.Exp,
                                      scale=np.float32(-0.5)),
            [bST[d_t1]], [bST[d_r]])
        vop(lambda e, s: e.scalar_tensor_tensor(
            stc(d_nmr), stc(d_m), -1.0, stc(d_r), ALU.mult, ALU.mult),
            [bST[d_m], bST[d_r]], [bST[d_nmr]])
        sop(lambda e, s, _x=xslot: e.activation(_x, _x, AF.Identity,
                                                bias=stc(d_nmr), scale=stc(d_r)),
            [bST[d_nmr], bST[d_r], bslot], [bslot])

    def s_phase(i):
        L = 128 * (i + 1)
        nkt = (L + 511) // 512
        xs = A5[:, 1024 * (i % 2):1024 * (i % 2) + 1024]
        dma("sync", xs, x_h[128 * i:128 * i + 128, :], [], [bA5[i % 2]])
        for kt in range(nkt):
            w = min(512, L - 512 * kt)
            sb_ = kt % 2
            mm(psum[sb_][:, 0:w], qT[:, 128 * i:128 * i + 128],
               kT[:, 512 * kt:512 * kt + w], True, True,
               [bqT, bkT], [bPS[sb_]])
            last = (kt == nkt - 1)
            if last:
                if w > 128:
                    vop(lambda e, s, _kt=kt, _sb=sb_, _w=w: e.tensor_copy(
                        Pbuf[:, 512 * _kt:512 * _kt + _w - 128],
                        psum[_sb][:, 0:_w - 128]),
                        [bPS[sb_]], [bA2s[kt]])
                vop(lambda e, s, _kt=kt, _sb=sb_, _w=w: e.tensor_add(
                    Pbuf[:, 512 * _kt + _w - 128:512 * _kt + _w],
                    psum[_sb][:, _w - 128:_w], mask128),
                    [bPS[sb_], bmask], [bA2s[kt]])
            else:
                vop(lambda e, s, _kt=kt, _sb=sb_: e.tensor_copy(
                    Pbuf[:, 512 * _kt:512 * _kt + 512], psum[_sb][:, 0:512]),
                    [bPS[sb_]], [bA2s[kt]])
            if kt == 0:
                vop(lambda e, s, _kt=kt, _w=w: e.reduce_max(
                    stc(c_max), Pbuf[:, 512 * _kt:512 * _kt + _w], AX.X),
                    [bA2s[kt]], [bST[c_max]])
            else:
                vop(lambda e, s, _kt=kt, _w=w: e.reduce_max(
                    stc(c_tm), Pbuf[:, 512 * _kt:512 * _kt + _w], AX.X),
                    [bA2s[kt]], [bST[c_tm]])
                vop(lambda e, s: e.tensor_max(stc(c_max), stc(c_max), stc(c_tm)),
                    [bST[c_max], bST[c_tm]], [bST[c_max]])
        vop(lambda e, s: e.tensor_scalar_mul(stc(c_nm8), stc(c_max), -QSCALE),
            [bST[c_max]], [bST[c_nm8]])
        for kt in range(nkt):
            w = min(512, L - 512 * kt)
            acc = c_es if kt == 0 else c_tm
            sop(lambda e, s, _kt=kt, _w=w, _a=acc: e.activation(
                Pbuf[:, 512 * _kt:512 * _kt + _w],
                Pbuf[:, 512 * _kt:512 * _kt + _w], AF.Exp,
                bias=stc(c_nm8), scale=stc(c_q8), accum_out=stc(_a)),
                [bST[c_nm8], bST[c_q8], bA2s[kt]], [bA2s[kt], bST[acc]])
            if kt > 0:
                vop(lambda e, s: e.tensor_add(stc(c_es), stc(c_es), stc(c_tm)),
                    [bST[c_es], bST[c_tm]], [bST[c_es]])
        vop(lambda e, s: e.reciprocal(stc(c_rs), stc(c_es)),
            [bST[c_es]], [bST[c_rs]])

    def pv_phase(i):
        ob = (4, 5)

        def ptrans(kc):
            tb = 2 + (kc % 2)
            tr(psum[tb][:, 0:128], Pbuf[:, 128 * kc:128 * kc + 128], ident[:],
               [bA2s[kc // 4], bident], [bPS[tb]])
            vop(lambda e, s, _kc=kc, _tb=tb: e.tensor_copy(
                A7[:, 128 * (_kc % 4):128 * (_kc % 4) + 128], psum[_tb][:, 0:128]),
                [bPS[tb]], [bA7[kc % 4]])

        ptrans(0)
        if i >= 1:
            ptrans(1)
        for kc in range(i + 1):
            for hh in range(2):
                mm(psum[ob[hh]][:, 0:512],
                   A7[:, 128 * (kc % 4):128 * (kc % 4) + 128],
                   A1[:, 1024 * kc + 512 * hh:1024 * kc + 512 * hh + 512],
                   kc == 0, kc == i,
                   [bA7[kc % 4], bA1[(1024 * kc + 512 * hh) // 512]],
                   [bPS[ob[hh]]])
            if kc + 2 <= i:
                ptrans(kc + 2)
        xs = A5[:, 1024 * (i % 2):1024 * (i % 2) + 1024]
        for hh in range(2):
            vop(lambda e, s, _hh=hh, _b=ob[hh], _xs=xs:
                e.scalar_tensor_tensor(
                    _xs[:, 512 * _hh:512 * _hh + 512], psum[_b][:, 0:512],
                    stc(c_rs), _xs[:, 512 * _hh:512 * _hh + 512],
                    ALU.mult, ALU.add),
                [bPS[ob[hh]], bST[c_rs], bA5[i % 2]], [bA5[i % 2]])
        dma("gpsimd", xa_h[128 * i:128 * i + 128, :], xs, [bA5[i % 2]],
            [bHBM["xa"]])
        emit_ln2p(xs, bA5[i % 2])
        dma("gpsimd", y2_h[128 * i:128 * i + 128, :], xs, [bA5[i % 2]],
            [by2[i]])

    for i in range(NT):
        s_phase(i)
        pv_phase(i)

    P.barrier()

    sbpr = AW[0:1, 5120:6144]
    bsbpr = Buf("sbpr")
    bprT = A2[:, 4224:5248]
    dma("sync", sbpr, bass.AP(bpr_h.tensor, 0, [[D, 1], [1, D]]), [], [bsbpr])
    for hh in range(2):
        mm(psum[4 + hh][:, 0:512], sones[:], sbpr[:, 512 * hh:512 * hh + 512],
           True, True, [bsones, bsbpr], [bPS[4 + hh]])
        vop(lambda e, s, _hh=hh: e.tensor_copy(
            bprT[:, 512 * _hh:512 * _hh + 512], psum[4 + _hh][:, 0:512]),
            [bPS[4 + hh]], [bbprT])

    w1s = [A4[:, 0:1024], A4[:, 1024:2048]]
    bw1s = [Buf("w1s0"), Buf("w1s1")]
    hbuf = AW[:, 0:4096]
    bhbuf = Buf("hb")
    NPAN_T = T // 512

    def gen_p3():
        dma("sync", A5[:, 0:1024], y2_h[0:128, :], [by2[0]], [bA5[0]])
        for t in range(NT):
            xs = A5[:, 1024 * (t % 2):1024 * (t % 2) + 1024]
            if t + 1 < NT:
                dma("sync",
                    A5[:, 1024 * ((t + 1) % 2):1024 * ((t + 1) % 2) + 1024],
                    y2_h[128 * (t + 1):128 * (t + 1) + 128, :],
                    [by2[t + 1]], [bA5[(t + 1) % 2]])
            for c in range(ND):
                tb = c % 2
                tr(psum[tb][:, 0:128], xs[:, 128 * c:128 * c + 128], ident[:],
                   [bA5[t % 2], bident], [bPS[tb]])
                vop(lambda e, s, _c=c, _t=t, _tb=tb: e.tensor_copy(
                    A1[:, 4096 * _c + 128 * _t:4096 * _c + 128 * _t + 128],
                    psum[_tb][:, 0:128]),
                    [bPS[tb]], [bA1[(4096 * c + 128 * t) // 512]])
            if t % 4 == 3:
                yield

    def gen_fc():
        for fc in range(NF):
            dma("sync", w1s[fc % 2][:], w1t_h[128 * fc:128 * fc + 128, :],
                [bHBM["w1t"]], [bw1s[fc % 2]])
            for p in range(NPAN_T):
                hb = 2 + (fc * NPAN_T + p) % 2
                for c in range(ND):
                    mm(psum[hb][:, 0:512], w1s[fc % 2][:, 128 * c:128 * c + 128],
                       A1[:, 4096 * c + 512 * p:4096 * c + 512 * p + 512],
                       c == 0, c == ND - 1,
                       [bw1s[fc % 2], bA1[(4096 * c + 512 * p) // 512]],
                       [bPS[hb]])
                sop(lambda e, s, _fc=fc, _p=p, _hb=hb: e.activation(
                    hbuf[:, 512 * _p:512 * _p + 512], psum[_hb][:, 0:512],
                    AF.Gelu, bias=sbfc[:, _fc:_fc + 1]),
                    [bPS[hb], bsbfc], [bhbuf])
                if fc == 0:
                    yield
            dma("gpsimd", hT_h[128 * fc:128 * fc + 128, 0:T],
                hbuf[:, 0:T], [bhbuf], [bhT[fc]])

    g3, gf = gen_p3(), gen_fc()
    a3_, af_ = True, True
    while a3_ or af_:
        if a3_:
            a3_ = next(g3, _SENT) is not _SENT
        if af_:
            af_ = next(gf, _SENT) is not _SENT


    w2s = [A2[:, 1024 * j:1024 * j + 1024] for j in range(4)] + \
          [A4[:, 0:1024], A4[:, 1024:2048]]
    bw2s = [Buf(f"w2s{j}") for j in range(6)]
    hts = [AW[:, 4096 + 256 * j:4096 + 256 * j + 256] for j in range(4)]
    bhts = [Buf(f"hts{j}") for j in range(4)]

    for sp in range(NSUP):
        pb = 4 * ((sp + 1) % 2)
        for tt in range(2):
            t = 2 * sp + tt
            xs = A5[:, 1024 * tt:1024 * tt + 1024]
            dma("gpsimd", xs, xa_h[128 * t:128 * t + 128, :], [], [bA5[tt]],
                extra=[bHBM["xa"].writer])
        for fc in range(NF):
            j = fc % 6
            dma("sync" if fc % 2 == 0 else "scalar",
                w2s[j][:], w2t_h[128 * fc:128 * fc + 128, :],
                [], [bw2s[j]] + ([bw1s[j - 4]] if j >= 4 else []),
                extra=[bHBM["w2t"].writer])
            dma("sync", hts[fc % 4][:],
                hT_h[128 * fc:128 * fc + 128, 256 * sp:256 * sp + 256],
                [bhT[fc]], [bhts[fc % 4]])
            for tt in range(2):
                for hh in range(2):
                    mm(psum[pb + 2 * tt + hh][:, 0:512],
                       hts[fc % 4][:, 128 * tt:128 * tt + 128],
                       w2s[j][:, 512 * hh:512 * hh + 512],
                       fc == 0, fc == NF - 1,
                       [bhts[fc % 4], bw2s[j]], [bPS[pb + 2 * tt + hh]])
        for tt in range(2):
            t = 2 * sp + tt
            xs = A5[:, 1024 * tt:1024 * tt + 1024]
            for hh in range(2):
                bk = pb + 2 * tt + hh
                vop(lambda e, s, _bk=bk, _xs=xs, _hh=hh: e.tensor_add(
                    _xs[:, 512 * _hh:512 * _hh + 512], psum[_bk][:, 0:512],
                    _xs[:, 512 * _hh:512 * _hh + 512]),
                    [bPS[bk], bA5[tt]], [bA5[tt]])
                vop(lambda e, s, _bk=bk, _xs=xs, _hh=hh: e.tensor_add(
                    _xs[:, 512 * _hh:512 * _hh + 512],
                    _xs[:, 512 * _hh:512 * _hh + 512],
                    bprT[:, 512 * _hh:512 * _hh + 512]),
                    [bA5[tt], bbprT], [bA5[tt]])
            dma("gpsimd", y_h[128 * t:128 * t + 128, :], xs, [bA5[tt]],
                [bHBM["y"]])

    final_st = [(s, P.ctr[s]) for s in Prog.DPOOL["gpsimd"] if P.ctr[s] > 0]

    with es:
        sems = {}
        for s in P.ctr:
            sems[s] = es.enter_context(nc.semaphore(s))
        with nc.Block() as block:

            @block.tensor
            def _(eng):
                for run in P.sched["tensor"]:
                    run(eng, sems)

            @block.vector
            def _(eng):
                for run in P.sched["vector"]:
                    run(eng, sems)

            @block.scalar
            def _(eng):
                for run in P.sched["scalar"]:
                    run(eng, sems)

            @block.sync
            def _(eng):
                for run in P.sched["sync"]:
                    run(eng, sems)

            @block.gpsimd
            def _(eng):
                for run in P.sched["gpsimd"]:
                    run(eng, sems)
                for s, v in final_st:
                    eng.wait_ge(sems[s], v)

    return nc


def make_masks():
    col = np.arange(128, dtype=np.int64)[None, :]
    row = np.arange(128, dtype=np.int64)[:, None]
    return np.where(col <= row, np.float32(0.0), np.float32(NEG)).astype(np.float32)


def kernel(L0_attn_X, L0_attn_Wq, L0_attn_Wk, L0_attn_Wv,
           L0_ffn_W_fc, L0_ffn_b_fc, L0_ffn_A_fc, L0_ffn_B_fc,
           L0_ffn_W_proj, L0_ffn_b_proj, L0_ffn_A_proj, L0_ffn_B_proj):
    X = np.ascontiguousarray(np.asarray(L0_attn_X, dtype=np.float32))
    B, T, _ = X.shape
    assert B == 8 and T % 512 == 0

    nc = build_program(T)
    common = {
        "wq": np.ascontiguousarray(L0_attn_Wq, dtype=np.float32),
        "wk": np.ascontiguousarray(L0_attn_Wk, dtype=np.float32),
        "wv": np.ascontiguousarray(L0_attn_Wv, dtype=np.float32),
        "wfc": np.ascontiguousarray(L0_ffn_W_fc, dtype=np.float32),
        "bfc": np.ascontiguousarray(L0_ffn_b_fc, dtype=np.float32),
        "afc": np.ascontiguousarray(L0_ffn_A_fc, dtype=np.float32),
        "bfcB": np.ascontiguousarray(L0_ffn_B_fc, dtype=np.float32),
        "wpr": np.ascontiguousarray(L0_ffn_W_proj, dtype=np.float32),
        "bpr": np.ascontiguousarray(L0_ffn_b_proj, dtype=np.float32),
        "apr": np.ascontiguousarray(L0_ffn_A_proj, dtype=np.float32),
        "bprB": np.ascontiguousarray(L0_ffn_B_proj, dtype=np.float32),
        "ident": np.eye(128, dtype=np.float32),
        "cmask": make_masks(),
    }
    in_maps = [dict(common, x=np.ascontiguousarray(X[i])) for i in range(8)]
    res = run_bass_kernel_spmd(nc, in_maps, core_ids=list(range(8)))
    out = np.stack([np.asarray(r["y"], dtype=np.float32) for r in res.results])
    return out


# version 24
# speedup vs baseline: 1.0098x; 1.0098x over previous
import contextlib
import numpy as np
import concourse.bass as bass
import concourse.mybir as mybir
from concourse.bass_utils import run_bass_kernel_spmd

F32 = mybir.dt.float32
AF = mybir.ActivationFunctionType
ALU = mybir.AluOpType
AX = mybir.AxisListType

D = 1024
DQ = 64
FF = 4096
ND = D // 128
NF = FF // 128
EPS = 1e-5
NEG = -1.0e30
QSCALE = 0.125


class Buf:
    __slots__ = ("writer", "readers", "name")

    def __init__(self, name=""):
        self.writer = None
        self.readers = []
        self.name = name


class Prog:
    ENGINES = ("tensor", "vector", "scalar", "sync", "gpsimd")
    ESEM = {"tensor": "pe", "vector": "ve", "scalar": "se"}
    DPOOL = {"sync": [f"ld{i}" for i in range(12)],
             "gpsimd": [f"st{i}" for i in range(4)],
             "scalar": [f"sd{i}" for i in range(6)]}

    def __init__(self):
        self.sched = {e: [] for e in self.ENGINES}
        self.ctr = {s: 0 for s in ("pe", "ve", "se")}
        for pool in self.DPOOL.values():
            for s in pool:
                self.ctr[s] = 0
        self.dma_n = {e: 0 for e in self.DPOOL}
        self.wm = {e: {} for e in self.ENGINES}

    def _waits(self, eng, tokens):
        merged = {}
        for t in tokens:
            if t is None:
                continue
            s, v = t
            if eng == "tensor" and s == "pe":
                continue
            if merged.get(s, -1) < v:
                merged[s] = v
        out = []
        for s, v in merged.items():
            if self.wm[eng].get(s, 0) >= v:
                continue
            out.append((s, v))
            self.wm[eng][s] = v
        return out

    def emit(self, eng, build, reads=(), writes=(), dma=False, extra=()):
        toks = []
        for b in reads:
            toks.append(b.writer)
        for b in writes:
            toks.append(b.writer)
            toks.extend(b.readers)
        toks.extend(extra)

        if dma:
            pool = self.DPOOL[eng]
            sem = pool[self.dma_n[eng] % len(pool)]
            self.dma_n[eng] += 1
            if self.ctr[sem] > 0:
                toks.append((sem, self.ctr[sem]))
            amt = 16
        else:
            sem, amt = self.ESEM[eng], 1
        waits = self._waits(eng, toks)
        self.ctr[sem] += amt
        token = (sem, self.ctr[sem])

        def run(engobj, sems, _waits=waits, _build=build, _sem=sem, _amt=amt):
            for s, v in _waits:
                engobj.wait_ge(sems[s], v)
            ins = _build(engobj, sems)
            ins.then_inc(sems[_sem], _amt)

        self.sched[eng].append(run)

        for b in reads:
            b.readers.append(token)
        for b in writes:
            b.writer = token
            b.readers = []
        return token

    def barrier(self):
        snap = dict(self.ctr)
        for eng in self.ENGINES:
            waits = []
            for s, v in snap.items():
                if v > 0 and self.wm[eng].get(s, 0) < v:
                    waits.append((s, v))
                    self.wm[eng][s] = v
            if not waits:
                continue

            def run(engobj, sems, _waits=waits):
                for s, v in _waits:
                    engobj.wait_ge(sems[s], v)

            self.sched[eng].append(run)


def build_program(T):
    NT = T // 128
    NPAN = T // 256
    NSUP = NT // 2

    nc = bass.Bass("TRN2", target_bir_lowering=False)

    x_h = nc.dram_tensor("x", [T, D], F32, kind="ExternalInput").ap()
    wq_h = nc.dram_tensor("wq", [D, DQ], F32, kind="ExternalInput").ap()
    wk_h = nc.dram_tensor("wk", [D, DQ], F32, kind="ExternalInput").ap()
    wv_h = nc.dram_tensor("wv", [D, D], F32, kind="ExternalInput").ap()
    wfc_h = nc.dram_tensor("wfc", [FF, D], F32, kind="ExternalInput").ap()
    bfc_h = nc.dram_tensor("bfc", [FF], F32, kind="ExternalInput").ap()
    afc_h = nc.dram_tensor("afc", [16, D], F32, kind="ExternalInput").ap()
    bfcB_h = nc.dram_tensor("bfcB", [FF, 16], F32, kind="ExternalInput").ap()
    wpr_h = nc.dram_tensor("wpr", [D, FF], F32, kind="ExternalInput").ap()
    bpr_h = nc.dram_tensor("bpr", [D], F32, kind="ExternalInput").ap()
    apr_h = nc.dram_tensor("apr", [16, FF], F32, kind="ExternalInput").ap()
    bprB_h = nc.dram_tensor("bprB", [D, 16], F32, kind="ExternalInput").ap()
    ident_h = nc.dram_tensor("ident", [128, 128], F32, kind="ExternalInput").ap()
    cmask_h = nc.dram_tensor("cmask", [128, 128], F32, kind="ExternalInput").ap()

    y_h = nc.dram_tensor("y", [T, D], F32, kind="ExternalOutput").ap()
    y2_h = nc.dram_tensor("y2", [T, D], F32, kind="Internal").ap()
    xa_h = nc.dram_tensor("xa", [T, D], F32, kind="Internal").ap()
    hT_h = nc.dram_tensor("hT", [FF, T], F32, kind="Internal").ap()
    w1t_h = nc.dram_tensor("w1t", [NF * 128, D], F32, kind="Internal").ap()
    w2t_h = nc.dram_tensor("w2t", [NF * 128, D], F32, kind="Internal").ap()
    yt_h = nc.dram_tensor("yt", [T // 256 * 128, 2048], F32, kind="Internal").ap()

    es = contextlib.ExitStack()
    sb = lambda name, shape: es.enter_context(nc.sbuf_tensor(name, shape, F32))

    A1 = sb("A1", [128, 32768])
    AW = sb("AW", [128, 8192])
    A2 = sb("A2", [128, 6272])
    A4 = sb("A4", [128, 2048])
    A5 = sb("A5", [128, 2048])
    A7 = sb("A7", [128, 512])
    ident = sb("identt", [128, 128])
    swq = sb("swq", [128, 8 * DQ])
    swk = sb("swk", [128, 8 * DQ])
    sbfc = sb("sbfc", [128, 32])
    sones = sb("sones", [1, 128])
    ST = sb("stats", [128, 32])

    c_sum, c_sq, c_m, c_var, c_r, c_nmr = 0, 1, 2, 3, 4, 5
    c_max, c_tm, c_es, c_rs, c_nm8 = 6, 7, 8, 9, 10
    c_wmax, c_ws1, c_ws2, c_tmp = 11, 12, 13, 14
    c_eps, c_q8 = 15, 16
    c_one, c_cg, c_wsi = 17, 18, 19
    c_ptm, c_ptmp = 20, 21
    d_sum, d_sq, d_m, d_var, d_r, d_nmr, d_t1, d_t2 = 24, 25, 26, 27, 28, 29, 30, 31
    C32 = 12582912.0
    C64 = 6755399441055744.0
    E30 = 2.0 ** -30

    def stc(i):
        return ST[:, i:i + 1]

    psum = [es.enter_context(nc.psum_tensor(f"ps{i}", [128, 512], F32))
            for i in range(8)]

    P = Prog()

    bA1 = [Buf(f"A1_{i}") for i in range(64)]
    bAW = [Buf(f"AW_{i}") for i in range(16)]
    bA2s = [Buf(f"A2P_{i}") for i in range(8)]
    bmask = Buf("mask")
    bbprT = Buf("bprT")
    bA4 = [Buf(f"A4_{i}") for i in range(4)]
    bA5 = [Buf("A5_0"), Buf("A5_1")]
    bA7 = [Buf(f"A7_{i}") for i in range(4)]
    bPS = [Buf(f"ps{i}") for i in range(8)]
    bident = Buf("ident")
    bswq, bswk = Buf("swq"), Buf("swk")
    bsbfc = Buf("sbfc")
    bsones = Buf("sones")
    bST = [Buf(f"st{i}") for i in range(32)]
    bHBM = {k: Buf(k) for k in ("xa", "hT", "w1t", "w2t", "y")}
    bhT = [Buf(f"hT{i}") for i in range(NF)]
    byt = [Buf(f"yt{i}") for i in range(T // 256)]
    by2 = [Buf(f"y2_{i}") for i in range(NT)]

    def dma(eng, dst, src, reads, writes, extra=()):
        return P.emit(eng, lambda e, s, _d=dst, _s=src: e.dma_start(out=_d, in_=_s),
                      reads=reads, writes=writes, dma=True, extra=extra)

    def mm(out, lhsT, rhs, start, stop, reads, writes, extra=(), is_tr=None):
        return P.emit("tensor",
                      lambda e, s, _o=out, _l=lhsT, _r=rhs, _st=start, _sp=stop,
                      _tr=is_tr:
                      e.matmul(_o, _l, _r, start=_st, stop=_sp, is_transpose=_tr),
                      reads=reads, writes=writes, extra=extra)

    def tr(out, in_, idn, reads, writes, extra=()):
        return P.emit("tensor",
                      lambda e, s, _o=out, _i=in_, _d=idn: e.transpose(_o, _i, _d),
                      reads=reads, writes=writes, extra=extra)

    def vop(fn, reads, writes, extra=()):
        return P.emit("vector", fn, reads=reads, writes=writes, extra=extra)

    def sop(fn, reads, writes, extra=()):
        return P.emit("scalar", fn, reads=reads, writes=writes, extra=extra)

    dma("sync", ident[:], ident_h, [], [bident])
    swq_src = bass.AP(wq_h.tensor, 0, [[DQ, 128], [128 * DQ, ND], [1, DQ]])
    swk_src = bass.AP(wk_h.tensor, 0, [[DQ, 128], [128 * DQ, ND], [1, DQ]])
    dma("sync", swq[:], swq_src, [], [bswq])
    dma("sync", swk[:], swk_src, [], [bswk])
    P.emit("vector", lambda e, s: e.memset(sones[:], 1.0), [], [bsones])
    P.emit("vector", lambda e, s: e.memset(ST[:, 15:16], EPS), [], [bST[15]])
    P.emit("vector", lambda e, s: e.memset(ST[:, 16:17], QSCALE), [], [bST[16]])
    P.emit("vector", lambda e, s: e.memset(stc(c_one), 1.0), [], [bST[c_one]])
    P.emit("vector", lambda e, s: e.memset(stc(c_tmp), E30), [], [bST[c_tmp]])
    P.emit("vector", lambda e, s: e.tensor_add(stc(c_tmp), stc(c_tmp), stc(c_one)),
           [bST[c_one], bST[c_tmp]], [bST[c_tmp]])
    P.emit("vector", lambda e, s: e.tensor_sub(stc(c_tmp), stc(c_tmp), stc(c_one)),
           [bST[c_one], bST[c_tmp]], [bST[c_tmp]])
    P.emit("vector", lambda e, s: e.tensor_scalar_mul(stc(c_tmp), stc(c_tmp), 2.0 ** 30),
           [bST[c_tmp]], [bST[c_tmp]])
    P.emit("vector", lambda e, s: e.memset(stc(c_cg), C32), [], [bST[c_cg]])
    P.emit("vector", lambda e, s: e.scalar_tensor_tensor(
        stc(c_cg), stc(c_tmp), C64, stc(c_cg), ALU.mult, ALU.add),
        [bST[c_tmp], bST[c_cg]], [bST[c_cg]])

    bfc_raw = A4[0:32, 640:768]
    dma("sync", bfc_raw, bass.AP(bfc_h.tensor, 0, [[128, 32], [1, 128]]),
        [], [bA4[1]])
    tr(psum[4][0:128, 0:32], bfc_raw, ident[0:32, 0:32], [bA4[1], bident], [bPS[4]])
    vop(lambda e, s: e.tensor_copy(sbfc[:], psum[4][0:128, 0:32]), [bPS[4]], [bsbfc])

    pL = [A2[:, 0:1024], A2[:, 1024:2048], A2[:, 5248:6272]]
    bpL = [Buf("pL0"), Buf("pL1"), Buf("pL2")]
    pqa = A2[:, 2048:3072]
    bpqa = Buf("pqa")
    pstage = A2[:, 3072:4096]
    bpstage = Buf("pstage")
    aCh = A2[0:16, 4096:5120]
    baCh = Buf("aCh")
    bTs = [A7[0:16, 0:128], A7[0:16, 128:256]]
    bbT = [Buf("bT0"), Buf("bT1")]
    praw = [A7[:, 256:272], A7[:, 272:288]]
    bpraw = [Buf("praw0"), Buf("praw1")]

    def gen_prep():
        for wname, w_h, b_h, a_h, n_tiles, tile_cols, c_ws in (
                ("w1", wfc_h, bfcB_h, afc_h, NF, D, c_ws1),
                ("w2", wpr_h, bprB_h, apr_h, ND, FF, c_ws2)):
            nq = tile_cols // 1024
            for u in range(n_tiles * nq):
                t, q = u // nq, u % nq
                sl = pL[u % 3]
                dma("scalar", sl, w_h[128 * t:128 * t + 128,
                                      1024 * q:1024 * q + 1024], [], [bpL[u % 3]])
                if u == 0:
                    vop(lambda e, s, _sl=sl: e.tensor_reduce(
                        stc(c_wmax), _sl, AX.X, ALU.max, apply_absolute_value=True),
                        [bpL[u % 3]], [bST[c_wmax]])
                else:
                    vop(lambda e, s, _sl=sl: e.tensor_reduce(
                        stc(c_ptm), _sl, AX.X, ALU.max, apply_absolute_value=True),
                        [bpL[u % 3]], [bST[c_ptm]])
                    vop(lambda e, s: e.tensor_max(stc(c_wmax), stc(c_wmax),
                                                  stc(c_ptm)),
                        [bST[c_ptm], bST[c_wmax]], [bST[c_wmax]])
                yield
            tr(psum[4][0:1, 0:128], stc(c_wmax), ident[:],
               [bST[c_wmax], bident], [bPS[4]])
            vop(lambda e, s: e.reduce_max(ST[0:1, c_ptmp:c_ptmp + 1],
                                          psum[4][0:1, 0:128], AX.X),
                [bPS[4]], [bST[c_ptmp]])
            vop(lambda e, s: e.tensor_scalar_mul(
                ST[0:1, c_ptmp:c_ptmp + 1], ST[0:1, c_ptmp:c_ptmp + 1],
                1.0 / 127.0), [bST[c_ptmp]], [bST[c_ptmp]])
            mm(psum[4][:, 0:1], sones[:], ST[0:1, c_ptmp:c_ptmp + 1], True, True,
               [bsones, bST[c_ptmp]], [bPS[4]])
            vop(lambda e, s, _c=c_ws: e.tensor_copy(stc(_c), psum[4][:, 0:1]),
                [bPS[4]], [bST[c_ws]])
            vop(lambda e, s, _c=c_ws: e.reciprocal(stc(c_wsi), stc(_c)),
                [bST[c_ws]], [bST[c_wsi]])
            yield
            for q in range(nq):
                if wname == "w1":
                    if q == 0:
                        dma("scalar", aCh, a_h, [], [baCh])
                else:
                    dma("scalar", aCh, a_h[:, 1024 * q:1024 * q + 1024], [], [baCh])
                for t in range(n_tiles):
                    u = q * n_tiles + t
                    rs = praw[u % 2]
                    dma("scalar", rs, b_h[128 * t:128 * t + 128, :], [], [bpraw[u % 2]])
                    tr(psum[4][0:16, 0:128], rs, ident[:],
                       [bpraw[u % 2], bident], [bPS[4]])
                    vop(lambda e, s, _u=u: e.tensor_scalar_mul(
                        bTs[_u % 2], psum[4][0:16, 0:128], 2.0),
                        [bPS[4]], [bbT[u % 2]])
                    sl = pL[u % 3]
                    dma("scalar", sl, w_h[128 * t:128 * t + 128,
                                          1024 * q:1024 * q + 1024], [], [bpL[u % 3]])
                    vop(lambda e, s, _sl=sl: e.tensor_single_scalar(
                        pqa, _sl, stc(c_wsi), ALU.mult),
                        [bpL[u % 3], bST[c_wsi]], [bpqa])
                    vop(lambda e, s: e.tensor_scalar(
                        pqa, pqa, stc(c_cg), stc(c_cg), ALU.add, ALU.subtract),
                        [bpqa, bST[c_cg]], [bpqa])
                    vop(lambda e, s, _c=c_ws: e.tensor_single_scalar(
                        pqa, pqa, stc(_c), ALU.mult),
                        [bpqa, bST[c_ws]], [bpqa])
                    for cc in range(8):
                        tb = 5 + (cc % 2)
                        mm(psum[tb][:, 0:128], pqa[:, 128 * cc:128 * cc + 128],
                           ident[:], True, False,
                           [bpqa, bident], [bPS[tb]], is_tr=True)
                        mm(psum[tb][:, 0:128], aCh[:, 128 * cc:128 * cc + 128],
                           bTs[u % 2], False, True,
                           [baCh, bbT[u % 2]], [bPS[tb]])
                        sop(lambda e, s, _cc=cc, _tb=tb: e.activation(
                            pstage[:, 128 * _cc:128 * _cc + 128],
                            psum[_tb][:, 0:128], AF.Identity),
                            [bPS[tb]], [bpstage])
                    if wname == "w1":
                        dma("gpsimd", w1t_h[128 * t:128 * t + 128, :],
                            pstage[:], [bpstage], [bHBM["w1t"]])
                    else:
                        dst = bass.AP(w2t_h.tensor, 128 * t + 1024 * q * D,
                                      [[D, 128], [128 * D, 8], [1, 128]])
                        dma("gpsimd", dst, pstage[:], [bpstage], [bHBM["w2t"]])
                    yield

    P.barrier()

    def emit_ln(xslot, bslot):
        vop(lambda e, s, _x=xslot: e.reduce_sum(stc(c_sum), _x, AX.X),
            [bslot], [bST[c_sum]])
        sop(lambda e, s, _x=xslot: e.activation(psum[7][:, 0:512], _x[:, 0:512],
                                                AF.Square, accum_out=stc(c_sq)),
            [bslot], [bPS[7], bST[c_sq]])
        sop(lambda e, s, _x=xslot: e.activation(psum[7][:, 0:512], _x[:, 512:1024],
                                                AF.Square, accum_out=stc(c_tm)),
            [bslot], [bPS[7], bST[c_tm]])
        vop(lambda e, s: e.tensor_add(stc(c_sq), stc(c_sq), stc(c_tm)),
            [bST[c_sq], bST[c_tm]], [bST[c_sq]])
        vop(lambda e, s: e.tensor_scalar_mul(stc(c_m), stc(c_sum), 1.0 / D),
            [bST[c_sum]], [bST[c_m]])
        vop(lambda e, s: e.tensor_mul(stc(c_tmp), stc(c_m), stc(c_m)),
            [bST[c_m]], [bST[c_tmp]])
        vop(lambda e, s: e.scalar_tensor_tensor(
            stc(c_var), stc(c_sq), 1.0 / D, stc(c_tmp), ALU.mult, ALU.subtract),
            [bST[c_sq], bST[c_tmp]], [bST[c_var]])
        sop(lambda e, s: e.activation(stc(c_tm), stc(c_var), AF.Sqrt,
                                      bias=stc(c_eps)),
            [bST[c_var], bST[c_eps]], [bST[c_tm]])
        vop(lambda e, s: e.reciprocal(stc(c_r), stc(c_tm)),
            [bST[c_tm]], [bST[c_r]])
        vop(lambda e, s: e.scalar_tensor_tensor(
            stc(c_nmr), stc(c_m), -1.0, stc(c_r), ALU.mult, ALU.mult),
            [bST[c_m], bST[c_r]], [bST[c_nmr]])
        sop(lambda e, s, _x=xslot: e.activation(_x, _x, AF.Identity,
                                                bias=stc(c_nmr), scale=stc(c_r)),
            [bST[c_nmr], bST[c_r], bslot], [bslot])

    qT = AW[0:64, 0:4096]
    kT = AW[0:64, 4096:8192]
    bqT, bkT = Buf("qT"), Buf("kT")

    def gen_p1ab():
        wv_src = bass.AP(wv_h.tensor, 0, [[D, 128], [128 * D, ND], [1, D]])
        dma("sync", AW[:], wv_src, [], list(bAW))
        for p in range(NPAN):
            for tt in range(2):
                t = 2 * p + tt
                xs = A5[:, 1024 * (t % 2):1024 * (t % 2) + 1024]
                bxs = bA5[t % 2]
                dma("sync", xs, x_h[128 * t:128 * t + 128, :], [], [bxs])
                emit_ln(xs, bxs)
                dma("gpsimd", xa_h[128 * t:128 * t + 128, :], xs, [bxs],
                    [bHBM["xa"]])
                for c in range(ND):
                    tb = c % 2
                    tr(psum[tb][:, 0:128], xs[:, 128 * c:128 * c + 128], ident[:],
                       [bxs, bident], [bPS[tb]])
                    vop(lambda e, s, _c=c, _tt=tt, _tb=tb: e.tensor_copy(
                        A4[:, 256 * _c + 128 * _tt:256 * _c + 128 * _tt + 128],
                        psum[_tb][:, 0:128]),
                        [bPS[tb]], [bA4[(256 * c + 128 * tt) // 512]])
                for hh in range(2):
                    vb = 2 + hh
                    for c in range(ND):
                        mm(psum[vb][:, 0:512],
                           A4[:, 256 * c + 128 * tt:256 * c + 128 * tt + 128],
                           AW[:, 1024 * c + 512 * hh:1024 * c + 512 * hh + 512],
                           c == 0, c == ND - 1,
                           [bA4[(256 * c) // 512],
                            bAW[(1024 * c + 512 * hh) // 512]],
                           [bPS[vb]])
                    vop(lambda e, s, _t=t, _hh=hh, _vb=vb: e.tensor_copy(
                        A1[:, 1024 * _t + 512 * _hh:1024 * _t + 512 * _hh + 512],
                        psum[_vb][:, 0:512]),
                        [bPS[vb]], [bA1[(1024 * t + 512 * hh) // 512]])
                if tt == 1:
                    dma("gpsimd", yt_h[128 * p:128 * p + 128, :], A4[:],
                        [bA4[0], bA4[1], bA4[2], bA4[3]], [byt[p]])
                yield
        pnl = [A4, A5]
        bpnl = [[bA4[0], bA4[1], bA4[2], bA4[3]], [bA5[0], bA5[1]]]
        for p in range(NPAN):
            pb_ = pnl[p % 2]
            bb_ = bpnl[p % 2]
            dma("sync", pb_[:], yt_h[128 * p:128 * p + 128, :], [byt[p]], bb_)
            for qk in range(2):
                bank = 2 + qk
                wsb, bwsb = (swq, bswq) if qk == 0 else (swk, bswk)
                dst, bdst = (qT, bqT) if qk == 0 else (kT, bkT)
                for c in range(ND):
                    mm(psum[bank][0:64, 0:256], wsb[:, DQ * c:DQ * c + DQ],
                       pb_[:, 256 * c:256 * c + 256], c == 0, c == ND - 1,
                       [bwsb] + bb_, [bPS[bank]])
                aw_lo = (4096 * qk + 256 * p) // 512
                vop(lambda e, s, _p=p, _d=dst, _b=bank: e.tensor_copy(
                    _d[:, 256 * _p:256 * _p + 256], psum[_b][0:64, 0:256]),
                    [bPS[bank]], [bdst, bAW[aw_lo]])
            yield

    _SENT = object()
    gp, g1 = gen_prep(), gen_p1ab()
    ap_, a1_ = True, True
    while ap_ or a1_:
        for _ in range(3):
            if ap_:
                ap_ = next(gp, _SENT) is not _SENT
        if a1_:
            a1_ = next(g1, _SENT) is not _SENT

    P.barrier()

    Pbuf = A2[:, 0:4096]
    mask128 = A2[:, 4096:4224]
    dma("sync", mask128, cmask_h, [], [bmask])

    def emit_ln2p(xslot, bslot):
        vop(lambda e, s, _x=xslot: e.reduce_sum(stc(d_sum), _x, AX.X),
            [bslot], [bST[d_sum]])
        sop(lambda e, s, _x=xslot: e.activation(psum[6][:, 0:512], _x[:, 0:512],
                                                AF.Square, accum_out=stc(d_sq)),
            [bslot], [bPS[6], bST[d_sq]])
        sop(lambda e, s, _x=xslot: e.activation(psum[6][:, 0:512], _x[:, 512:1024],
                                                AF.Square, accum_out=stc(d_t1)),
            [bslot], [bPS[6], bST[d_t1]])
        vop(lambda e, s: e.tensor_add(stc(d_sq), stc(d_sq), stc(d_t1)),
            [bST[d_sq], bST[d_t1]], [bST[d_sq]])
        vop(lambda e, s: e.tensor_scalar_mul(stc(d_m), stc(d_sum), 1.0 / D),
            [bST[d_sum]], [bST[d_m]])
        vop(lambda e, s: e.tensor_mul(stc(d_t2), stc(d_m), stc(d_m)),
            [bST[d_m]], [bST[d_t2]])
        vop(lambda e, s: e.scalar_tensor_tensor(
            stc(d_var), stc(d_sq), 1.0 / D, stc(d_t2), ALU.mult, ALU.subtract),
            [bST[d_sq], bST[d_t2]], [bST[d_var]])
        sop(lambda e, s: e.activation(stc(d_t1), stc(d_var), AF.Ln,
                                      bias=stc(c_eps)),
            [bST[d_var], bST[c_eps]], [bST[d_t1]])
        sop(lambda e, s: e.activation(stc(d_r), stc(d_t1), AF.Exp,
                                      scale=np.float32(-0.5)),
            [bST[d_t1]], [bST[d_r]])
        vop(lambda e, s: e.scalar_tensor_tensor(
            stc(d_nmr), stc(d_m), -1.0, stc(d_r), ALU.mult, ALU.mult),
            [bST[d_m], bST[d_r]], [bST[d_nmr]])
        sop(lambda e, s, _x=xslot: e.activation(_x, _x, AF.Identity,
                                                bias=stc(d_nmr), scale=stc(d_r)),
            [bST[d_nmr], bST[d_r], bslot], [bslot])

    def s_phase(i):
        L = 128 * (i + 1)
        nkt = (L + 511) // 512
        xs = A5[:, 1024 * (i % 2):1024 * (i % 2) + 1024]
        dma("sync", xs, x_h[128 * i:128 * i + 128, :], [], [bA5[i % 2]])
        for kt in range(nkt):
            w = min(512, L - 512 * kt)
            sb_ = kt % 2
            mm(psum[sb_][:, 0:w], qT[:, 128 * i:128 * i + 128],
               kT[:, 512 * kt:512 * kt + w], True, True,
               [bqT, bkT], [bPS[sb_]])
            last = (kt == nkt - 1)
            if last:
                if w > 128:
                    vop(lambda e, s, _kt=kt, _sb=sb_, _w=w: e.tensor_copy(
                        Pbuf[:, 512 * _kt:512 * _kt + _w - 128],
                        psum[_sb][:, 0:_w - 128]),
                        [bPS[sb_]], [bA2s[kt]])
                vop(lambda e, s, _kt=kt, _sb=sb_, _w=w: e.tensor_add(
                    Pbuf[:, 512 * _kt + _w - 128:512 * _kt + _w],
                    psum[_sb][:, _w - 128:_w], mask128),
                    [bPS[sb_], bmask], [bA2s[kt]])
            else:
                vop(lambda e, s, _kt=kt, _sb=sb_: e.tensor_copy(
                    Pbuf[:, 512 * _kt:512 * _kt + 512], psum[_sb][:, 0:512]),
                    [bPS[sb_]], [bA2s[kt]])
            if kt == 0:
                vop(lambda e, s, _kt=kt, _w=w: e.reduce_max(
                    stc(c_max), Pbuf[:, 512 * _kt:512 * _kt + _w], AX.X),
                    [bA2s[kt]], [bST[c_max]])
            else:
                vop(lambda e, s, _kt=kt, _w=w: e.reduce_max(
                    stc(c_tm), Pbuf[:, 512 * _kt:512 * _kt + _w], AX.X),
                    [bA2s[kt]], [bST[c_tm]])
                vop(lambda e, s: e.tensor_max(stc(c_max), stc(c_max), stc(c_tm)),
                    [bST[c_max], bST[c_tm]], [bST[c_max]])
        vop(lambda e, s: e.tensor_scalar_mul(stc(c_nm8), stc(c_max), -QSCALE),
            [bST[c_max]], [bST[c_nm8]])
        for kt in range(nkt):
            w = min(512, L - 512 * kt)
            acc = c_es if kt == 0 else c_tm
            sop(lambda e, s, _kt=kt, _w=w, _a=acc: e.activation(
                Pbuf[:, 512 * _kt:512 * _kt + _w],
                Pbuf[:, 512 * _kt:512 * _kt + _w], AF.Exp,
                bias=stc(c_nm8), scale=stc(c_q8), accum_out=stc(_a)),
                [bST[c_nm8], bST[c_q8], bA2s[kt]], [bA2s[kt], bST[acc]])
            if kt > 0:
                vop(lambda e, s: e.tensor_add(stc(c_es), stc(c_es), stc(c_tm)),
                    [bST[c_es], bST[c_tm]], [bST[c_es]])
        vop(lambda e, s: e.reciprocal(stc(c_rs), stc(c_es)),
            [bST[c_es]], [bST[c_rs]])

    def pv_phase(i):
        ob = (4, 5)

        def ptrans(kc):
            tb = 2 + (kc % 2)
            tr(psum[tb][:, 0:128], Pbuf[:, 128 * kc:128 * kc + 128], ident[:],
               [bA2s[kc // 4], bident], [bPS[tb]])
            vop(lambda e, s, _kc=kc, _tb=tb: e.tensor_copy(
                A7[:, 128 * (_kc % 4):128 * (_kc % 4) + 128], psum[_tb][:, 0:128]),
                [bPS[tb]], [bA7[kc % 4]])

        ptrans(0)
        if i >= 1:
            ptrans(1)
        for kc in range(i + 1):
            if kc + 2 <= i:
                ptrans(kc + 2)
            for hh in range(2):
                mm(psum[ob[hh]][:, 0:512],
                   A7[:, 128 * (kc % 4):128 * (kc % 4) + 128],
                   A1[:, 1024 * kc + 512 * hh:1024 * kc + 512 * hh + 512],
                   kc == 0, kc == i,
                   [bA7[kc % 4], bA1[(1024 * kc + 512 * hh) // 512]],
                   [bPS[ob[hh]]])
        xs = A5[:, 1024 * (i % 2):1024 * (i % 2) + 1024]
        for hh in range(2):
            vop(lambda e, s, _hh=hh, _b=ob[hh], _xs=xs:
                e.scalar_tensor_tensor(
                    _xs[:, 512 * _hh:512 * _hh + 512], psum[_b][:, 0:512],
                    stc(c_rs), _xs[:, 512 * _hh:512 * _hh + 512],
                    ALU.mult, ALU.add),
                [bPS[ob[hh]], bST[c_rs], bA5[i % 2]], [bA5[i % 2]])
        dma("gpsimd", xa_h[128 * i:128 * i + 128, :], xs, [bA5[i % 2]],
            [bHBM["xa"]])
        emit_ln2p(xs, bA5[i % 2])
        dma("gpsimd", y2_h[128 * i:128 * i + 128, :], xs, [bA5[i % 2]],
            [by2[i]])

    for i in range(NT):
        s_phase(i)
        pv_phase(i)

    P.barrier()

    sbpr = AW[0:1, 5120:6144]
    bsbpr = Buf("sbpr")
    bprT = A2[:, 4224:5248]
    dma("sync", sbpr, bass.AP(bpr_h.tensor, 0, [[D, 1], [1, D]]), [], [bsbpr])
    for hh in range(2):
        mm(psum[4 + hh][:, 0:512], sones[:], sbpr[:, 512 * hh:512 * hh + 512],
           True, True, [bsones, bsbpr], [bPS[4 + hh]])
        vop(lambda e, s, _hh=hh: e.tensor_copy(
            bprT[:, 512 * _hh:512 * _hh + 512], psum[4 + _hh][:, 0:512]),
            [bPS[4 + hh]], [bbprT])

    w1s = [A4[:, 0:1024], A4[:, 1024:2048]]
    bw1s = [Buf("w1s0"), Buf("w1s1")]
    hbuf = AW[:, 0:4096]
    bhbuf = Buf("hb")
    NPAN_T = T // 512

    def gen_p3():
        dma("sync", A5[:, 0:1024], y2_h[0:128, :], [by2[0]], [bA5[0]])
        for t in range(NT):
            xs = A5[:, 1024 * (t % 2):1024 * (t % 2) + 1024]
            if t + 1 < NT:
                dma("sync",
                    A5[:, 1024 * ((t + 1) % 2):1024 * ((t + 1) % 2) + 1024],
                    y2_h[128 * (t + 1):128 * (t + 1) + 128, :],
                    [by2[t + 1]], [bA5[(t + 1) % 2]])
            for c in range(ND):
                tb = c % 2
                tr(psum[tb][:, 0:128], xs[:, 128 * c:128 * c + 128], ident[:],
                   [bA5[t % 2], bident], [bPS[tb]])
                vop(lambda e, s, _c=c, _t=t, _tb=tb: e.tensor_copy(
                    A1[:, 4096 * _c + 128 * _t:4096 * _c + 128 * _t + 128],
                    psum[_tb][:, 0:128]),
                    [bPS[tb]], [bA1[(4096 * c + 128 * t) // 512]])
            if t % 4 == 3:
                yield

    def gen_fc():
        for fc in range(NF):
            dma("sync", w1s[fc % 2][:], w1t_h[128 * fc:128 * fc + 128, :],
                [bHBM["w1t"]], [bw1s[fc % 2]])
            for p in range(NPAN_T):
                hb = 2 + (fc * NPAN_T + p) % 2
                for c in range(ND):
                    mm(psum[hb][:, 0:512], w1s[fc % 2][:, 128 * c:128 * c + 128],
                       A1[:, 4096 * c + 512 * p:4096 * c + 512 * p + 512],
                       c == 0, c == ND - 1,
                       [bw1s[fc % 2], bA1[(4096 * c + 512 * p) // 512]],
                       [bPS[hb]])
                sop(lambda e, s, _fc=fc, _p=p, _hb=hb: e.activation(
                    hbuf[:, 512 * _p:512 * _p + 512], psum[_hb][:, 0:512],
                    AF.Gelu, bias=sbfc[:, _fc:_fc + 1]),
                    [bPS[hb], bsbfc], [bhbuf])
                if fc == 0:
                    yield
            dma("gpsimd", hT_h[128 * fc:128 * fc + 128, 0:T],
                hbuf[:, 0:T], [bhbuf], [bhT[fc]])

    g3, gf = gen_p3(), gen_fc()
    a3_, af_ = True, True
    while a3_ or af_:
        if a3_:
            a3_ = next(g3, _SENT) is not _SENT
        if af_:
            af_ = next(gf, _SENT) is not _SENT


    w2s = [A2[:, 1024 * j:1024 * j + 1024] for j in range(4)] + \
          [A4[:, 0:1024], A4[:, 1024:2048]]
    bw2s = [Buf(f"w2s{j}") for j in range(6)]
    hts = [AW[:, 4096 + 256 * j:4096 + 256 * j + 256] for j in range(4)]
    bhts = [Buf(f"hts{j}") for j in range(4)]

    for sp in range(NSUP):
        pb = 4 * ((sp + 1) % 2)
        for tt in range(2):
            t = 2 * sp + tt
            xs = A5[:, 1024 * tt:1024 * tt + 1024]
            dma("gpsimd", xs, xa_h[128 * t:128 * t + 128, :], [], [bA5[tt]],
                extra=[bHBM["xa"].writer])
        for fc in range(NF):
            j = fc % 6
            dma("sync" if fc % 2 == 0 else "scalar",
                w2s[j][:], w2t_h[128 * fc:128 * fc + 128, :],
                [], [bw2s[j]] + ([bw1s[j - 4]] if j >= 4 else []),
                extra=[bHBM["w2t"].writer])
            dma("sync", hts[fc % 4][:],
                hT_h[128 * fc:128 * fc + 128, 256 * sp:256 * sp + 256],
                [bhT[fc]], [bhts[fc % 4]])
            for tt in range(2):
                for hh in range(2):
                    mm(psum[pb + 2 * tt + hh][:, 0:512],
                       hts[fc % 4][:, 128 * tt:128 * tt + 128],
                       w2s[j][:, 512 * hh:512 * hh + 512],
                       fc == 0, fc == NF - 1,
                       [bhts[fc % 4], bw2s[j]], [bPS[pb + 2 * tt + hh]])
        for tt in range(2):
            t = 2 * sp + tt
            xs = A5[:, 1024 * tt:1024 * tt + 1024]
            for hh in range(2):
                bk = pb + 2 * tt + hh
                vop(lambda e, s, _bk=bk, _xs=xs, _hh=hh: e.tensor_add(
                    _xs[:, 512 * _hh:512 * _hh + 512], psum[_bk][:, 0:512],
                    _xs[:, 512 * _hh:512 * _hh + 512]),
                    [bPS[bk], bA5[tt]], [bA5[tt]])
                vop(lambda e, s, _bk=bk, _xs=xs, _hh=hh: e.tensor_add(
                    _xs[:, 512 * _hh:512 * _hh + 512],
                    _xs[:, 512 * _hh:512 * _hh + 512],
                    bprT[:, 512 * _hh:512 * _hh + 512]),
                    [bA5[tt], bbprT], [bA5[tt]])
            dma("gpsimd", y_h[128 * t:128 * t + 128, :], xs, [bA5[tt]],
                [bHBM["y"]])

    final_st = [(s, P.ctr[s]) for s in Prog.DPOOL["gpsimd"] if P.ctr[s] > 0]

    with es:
        sems = {}
        for s in P.ctr:
            sems[s] = es.enter_context(nc.semaphore(s))
        with nc.Block() as block:

            @block.tensor
            def _(eng):
                for run in P.sched["tensor"]:
                    run(eng, sems)

            @block.vector
            def _(eng):
                for run in P.sched["vector"]:
                    run(eng, sems)

            @block.scalar
            def _(eng):
                for run in P.sched["scalar"]:
                    run(eng, sems)

            @block.sync
            def _(eng):
                for run in P.sched["sync"]:
                    run(eng, sems)

            @block.gpsimd
            def _(eng):
                for run in P.sched["gpsimd"]:
                    run(eng, sems)
                for s, v in final_st:
                    eng.wait_ge(sems[s], v)

    return nc


def make_masks():
    col = np.arange(128, dtype=np.int64)[None, :]
    row = np.arange(128, dtype=np.int64)[:, None]
    return np.where(col <= row, np.float32(0.0), np.float32(NEG)).astype(np.float32)


def kernel(L0_attn_X, L0_attn_Wq, L0_attn_Wk, L0_attn_Wv,
           L0_ffn_W_fc, L0_ffn_b_fc, L0_ffn_A_fc, L0_ffn_B_fc,
           L0_ffn_W_proj, L0_ffn_b_proj, L0_ffn_A_proj, L0_ffn_B_proj):
    X = np.ascontiguousarray(np.asarray(L0_attn_X, dtype=np.float32))
    B, T, _ = X.shape
    assert B == 8 and T % 512 == 0

    nc = build_program(T)
    common = {
        "wq": np.ascontiguousarray(L0_attn_Wq, dtype=np.float32),
        "wk": np.ascontiguousarray(L0_attn_Wk, dtype=np.float32),
        "wv": np.ascontiguousarray(L0_attn_Wv, dtype=np.float32),
        "wfc": np.ascontiguousarray(L0_ffn_W_fc, dtype=np.float32),
        "bfc": np.ascontiguousarray(L0_ffn_b_fc, dtype=np.float32),
        "afc": np.ascontiguousarray(L0_ffn_A_fc, dtype=np.float32),
        "bfcB": np.ascontiguousarray(L0_ffn_B_fc, dtype=np.float32),
        "wpr": np.ascontiguousarray(L0_ffn_W_proj, dtype=np.float32),
        "bpr": np.ascontiguousarray(L0_ffn_b_proj, dtype=np.float32),
        "apr": np.ascontiguousarray(L0_ffn_A_proj, dtype=np.float32),
        "bprB": np.ascontiguousarray(L0_ffn_B_proj, dtype=np.float32),
        "ident": np.eye(128, dtype=np.float32),
        "cmask": make_masks(),
    }
    in_maps = [dict(common, x=np.ascontiguousarray(X[i])) for i in range(8)]
    res = run_bass_kernel_spmd(nc, in_maps, core_ids=list(range(8)))
    out = np.stack([np.asarray(r["y"], dtype=np.float32) for r in res.results])
    return out


# version 25
# speedup vs baseline: 1.0108x; 1.0010x over previous
import contextlib
import numpy as np
import concourse.bass as bass
import concourse.mybir as mybir
from concourse.bass_utils import run_bass_kernel_spmd

F32 = mybir.dt.float32
AF = mybir.ActivationFunctionType
ALU = mybir.AluOpType
AX = mybir.AxisListType

D = 1024
DQ = 64
FF = 4096
ND = D // 128
NF = FF // 128
EPS = 1e-5
NEG = -1.0e30
QSCALE = 0.125


class Buf:
    __slots__ = ("writer", "readers", "name")

    def __init__(self, name=""):
        self.writer = None
        self.readers = []
        self.name = name


class Prog:
    ENGINES = ("tensor", "vector", "scalar", "sync", "gpsimd")
    ESEM = {"tensor": "pe", "vector": "ve", "scalar": "se"}
    DPOOL = {"sync": [f"ld{i}" for i in range(12)],
             "gpsimd": [f"st{i}" for i in range(4)],
             "scalar": [f"sd{i}" for i in range(6)]}

    def __init__(self):
        self.sched = {e: [] for e in self.ENGINES}
        self.ctr = {s: 0 for s in ("pe", "ve", "se")}
        for pool in self.DPOOL.values():
            for s in pool:
                self.ctr[s] = 0
        self.dma_n = {e: 0 for e in self.DPOOL}
        self.wm = {e: {} for e in self.ENGINES}

    def _waits(self, eng, tokens):
        merged = {}
        for t in tokens:
            if t is None:
                continue
            s, v = t
            if eng == "tensor" and s == "pe":
                continue
            if merged.get(s, -1) < v:
                merged[s] = v
        out = []
        for s, v in merged.items():
            if self.wm[eng].get(s, 0) >= v:
                continue
            out.append((s, v))
            self.wm[eng][s] = v
        return out

    def emit(self, eng, build, reads=(), writes=(), dma=False, extra=()):
        toks = []
        for b in reads:
            toks.append(b.writer)
        for b in writes:
            toks.append(b.writer)
            toks.extend(b.readers)
        toks.extend(extra)

        if dma:
            pool = self.DPOOL[eng]
            sem = pool[self.dma_n[eng] % len(pool)]
            self.dma_n[eng] += 1
            if self.ctr[sem] > 0:
                toks.append((sem, self.ctr[sem]))
            amt = 16
        else:
            sem, amt = self.ESEM[eng], 1
        waits = self._waits(eng, toks)
        self.ctr[sem] += amt
        token = (sem, self.ctr[sem])
        self.sched[eng].append(("i", build, waits, sem, self.ctr[sem], amt))

        for b in reads:
            b.readers.append(token)
        for b in writes:
            b.writer = token
            b.readers = []
        return token

    def finalize(self, extra_needed=()):
        needed = {}
        for s, v in extra_needed:
            needed.setdefault(s, set()).add(v)
        for entries in self.sched.values():
            for e in entries:
                waits = e[2] if e[0] == "i" else e[1]
                for s, v in waits:
                    needed.setdefault(s, set()).add(v)
        remap = {}
        for s, vals in needed.items():
            unit = 16 if s not in ("pe", "ve", "se", "ge") else 1
            new_of = {}
            for rank, v in enumerate(sorted(vals), start=1):
                new_of[v] = rank * unit
            remap[s] = new_of

        def map_wait(s, v):
            return remap[s][v]

        out = {}
        for eng, entries in self.sched.items():
            prog = []
            for e in entries:
                if e[0] == "i":
                    _, build, waits, sem, val, amt = e
                    keep = sem in remap and val in remap[sem]
                    nv = remap[sem][val] if keep else None
                    prog.append((build, [(s, map_wait(s, v)) for s, v in waits],
                                 sem if keep else None, amt))
                else:
                    prog.append((None, [(s, map_wait(s, v)) for s, v in e[1]],
                                 None, 0))
            out[eng] = prog
        self.remap = remap
        return out

    def barrier(self):
        snap = dict(self.ctr)
        for eng in self.ENGINES:
            waits = []
            for s, v in snap.items():
                if v > 0 and self.wm[eng].get(s, 0) < v:
                    waits.append((s, v))
                    self.wm[eng][s] = v
            if not waits:
                continue

            self.sched[eng].append(("w", waits))


def build_program(T):
    NT = T // 128
    NPAN = T // 256
    NSUP = NT // 2

    nc = bass.Bass("TRN2", target_bir_lowering=False)

    x_h = nc.dram_tensor("x", [T, D], F32, kind="ExternalInput").ap()
    wq_h = nc.dram_tensor("wq", [D, DQ], F32, kind="ExternalInput").ap()
    wk_h = nc.dram_tensor("wk", [D, DQ], F32, kind="ExternalInput").ap()
    wv_h = nc.dram_tensor("wv", [D, D], F32, kind="ExternalInput").ap()
    wfc_h = nc.dram_tensor("wfc", [FF, D], F32, kind="ExternalInput").ap()
    bfc_h = nc.dram_tensor("bfc", [FF], F32, kind="ExternalInput").ap()
    afc_h = nc.dram_tensor("afc", [16, D], F32, kind="ExternalInput").ap()
    bfcB_h = nc.dram_tensor("bfcB", [FF, 16], F32, kind="ExternalInput").ap()
    wpr_h = nc.dram_tensor("wpr", [D, FF], F32, kind="ExternalInput").ap()
    bpr_h = nc.dram_tensor("bpr", [D], F32, kind="ExternalInput").ap()
    apr_h = nc.dram_tensor("apr", [16, FF], F32, kind="ExternalInput").ap()
    bprB_h = nc.dram_tensor("bprB", [D, 16], F32, kind="ExternalInput").ap()
    ident_h = nc.dram_tensor("ident", [128, 128], F32, kind="ExternalInput").ap()
    cmask_h = nc.dram_tensor("cmask", [128, 128], F32, kind="ExternalInput").ap()

    y_h = nc.dram_tensor("y", [T, D], F32, kind="ExternalOutput").ap()
    y2_h = nc.dram_tensor("y2", [T, D], F32, kind="Internal").ap()
    xa_h = nc.dram_tensor("xa", [T, D], F32, kind="Internal").ap()
    hT_h = nc.dram_tensor("hT", [FF, T], F32, kind="Internal").ap()
    w1t_h = nc.dram_tensor("w1t", [NF * 128, D], F32, kind="Internal").ap()
    w2t_h = nc.dram_tensor("w2t", [NF * 128, D], F32, kind="Internal").ap()
    yt_h = nc.dram_tensor("yt", [T // 256 * 128, 2048], F32, kind="Internal").ap()

    es = contextlib.ExitStack()
    sb = lambda name, shape: es.enter_context(nc.sbuf_tensor(name, shape, F32))

    A1 = sb("A1", [128, 32768])
    AW = sb("AW", [128, 8192])
    A2 = sb("A2", [128, 6272])
    A4 = sb("A4", [128, 2048])
    A5 = sb("A5", [128, 2048])
    A7 = sb("A7", [128, 512])
    ident = sb("identt", [128, 128])
    swq = sb("swq", [128, 8 * DQ])
    swk = sb("swk", [128, 8 * DQ])
    sbfc = sb("sbfc", [128, 32])
    sones = sb("sones", [1, 128])
    ST = sb("stats", [128, 32])

    c_sum, c_sq, c_m, c_var, c_r, c_nmr = 0, 1, 2, 3, 4, 5
    c_max, c_tm, c_es, c_rs, c_nm8 = 6, 7, 8, 9, 10
    c_wmax, c_ws1, c_ws2, c_tmp = 11, 12, 13, 14
    c_eps, c_q8 = 15, 16
    c_one, c_cg, c_wsi = 17, 18, 19
    c_ptm, c_ptmp = 20, 21
    d_sum, d_sq, d_m, d_var, d_r, d_nmr, d_t1, d_t2 = 24, 25, 26, 27, 28, 29, 30, 31
    C32 = 12582912.0
    C64 = 6755399441055744.0
    E30 = 2.0 ** -30

    def stc(i):
        return ST[:, i:i + 1]

    psum = [es.enter_context(nc.psum_tensor(f"ps{i}", [128, 512], F32))
            for i in range(8)]

    P = Prog()

    bA1 = [Buf(f"A1_{i}") for i in range(64)]
    bAW = [Buf(f"AW_{i}") for i in range(16)]
    bA2s = [Buf(f"A2P_{i}") for i in range(8)]
    bmask = Buf("mask")
    bbprT = Buf("bprT")
    bA4 = [Buf(f"A4_{i}") for i in range(4)]
    bA5 = [Buf("A5_0"), Buf("A5_1")]
    bA7 = [Buf(f"A7_{i}") for i in range(4)]
    bPS = [Buf(f"ps{i}") for i in range(8)]
    bident = Buf("ident")
    bswq, bswk = Buf("swq"), Buf("swk")
    bsbfc = Buf("sbfc")
    bsones = Buf("sones")
    bST = [Buf(f"st{i}") for i in range(32)]
    bHBM = {k: Buf(k) for k in ("xa", "hT", "w1t", "w2t", "y")}
    bhT = [Buf(f"hT{i}") for i in range(NF)]
    byt = [Buf(f"yt{i}") for i in range(T // 256)]
    by2 = [Buf(f"y2_{i}") for i in range(NT)]

    def dma(eng, dst, src, reads, writes, extra=()):
        return P.emit(eng, lambda e, s, _d=dst, _s=src: e.dma_start(out=_d, in_=_s),
                      reads=reads, writes=writes, dma=True, extra=extra)

    def mm(out, lhsT, rhs, start, stop, reads, writes, extra=(), is_tr=None):
        return P.emit("tensor",
                      lambda e, s, _o=out, _l=lhsT, _r=rhs, _st=start, _sp=stop,
                      _tr=is_tr:
                      e.matmul(_o, _l, _r, start=_st, stop=_sp, is_transpose=_tr),
                      reads=reads, writes=writes, extra=extra)

    def tr(out, in_, idn, reads, writes, extra=()):
        return P.emit("tensor",
                      lambda e, s, _o=out, _i=in_, _d=idn: e.transpose(_o, _i, _d),
                      reads=reads, writes=writes, extra=extra)

    def vop(fn, reads, writes, extra=()):
        return P.emit("vector", fn, reads=reads, writes=writes, extra=extra)

    def sop(fn, reads, writes, extra=()):
        return P.emit("scalar", fn, reads=reads, writes=writes, extra=extra)

    dma("sync", ident[:], ident_h, [], [bident])
    swq_src = bass.AP(wq_h.tensor, 0, [[DQ, 128], [128 * DQ, ND], [1, DQ]])
    swk_src = bass.AP(wk_h.tensor, 0, [[DQ, 128], [128 * DQ, ND], [1, DQ]])
    dma("sync", swq[:], swq_src, [], [bswq])
    dma("sync", swk[:], swk_src, [], [bswk])
    P.emit("vector", lambda e, s: e.memset(sones[:], 1.0), [], [bsones])
    P.emit("vector", lambda e, s: e.memset(ST[:, 15:16], EPS), [], [bST[15]])
    P.emit("vector", lambda e, s: e.memset(ST[:, 16:17], QSCALE), [], [bST[16]])
    P.emit("vector", lambda e, s: e.memset(stc(c_one), 1.0), [], [bST[c_one]])
    P.emit("vector", lambda e, s: e.memset(stc(c_tmp), E30), [], [bST[c_tmp]])
    P.emit("vector", lambda e, s: e.tensor_add(stc(c_tmp), stc(c_tmp), stc(c_one)),
           [bST[c_one], bST[c_tmp]], [bST[c_tmp]])
    P.emit("vector", lambda e, s: e.tensor_sub(stc(c_tmp), stc(c_tmp), stc(c_one)),
           [bST[c_one], bST[c_tmp]], [bST[c_tmp]])
    P.emit("vector", lambda e, s: e.tensor_scalar_mul(stc(c_tmp), stc(c_tmp), 2.0 ** 30),
           [bST[c_tmp]], [bST[c_tmp]])
    P.emit("vector", lambda e, s: e.memset(stc(c_cg), C32), [], [bST[c_cg]])
    P.emit("vector", lambda e, s: e.scalar_tensor_tensor(
        stc(c_cg), stc(c_tmp), C64, stc(c_cg), ALU.mult, ALU.add),
        [bST[c_tmp], bST[c_cg]], [bST[c_cg]])

    bfc_raw = A4[0:32, 640:768]
    dma("sync", bfc_raw, bass.AP(bfc_h.tensor, 0, [[128, 32], [1, 128]]),
        [], [bA4[1]])
    tr(psum[4][0:128, 0:32], bfc_raw, ident[0:32, 0:32], [bA4[1], bident], [bPS[4]])
    vop(lambda e, s: e.tensor_copy(sbfc[:], psum[4][0:128, 0:32]), [bPS[4]], [bsbfc])

    pL = [A2[:, 0:1024], A2[:, 1024:2048], A2[:, 5248:6272]]
    bpL = [Buf("pL0"), Buf("pL1"), Buf("pL2")]
    pqa = A2[:, 2048:3072]
    bpqa = Buf("pqa")
    pstage = A2[:, 3072:4096]
    bpstage = Buf("pstage")
    aCh = A2[0:16, 4096:5120]
    baCh = Buf("aCh")
    bTs = [A7[0:16, 0:128], A7[0:16, 128:256]]
    bbT = [Buf("bT0"), Buf("bT1")]
    praw = [A7[:, 256:272], A7[:, 272:288]]
    bpraw = [Buf("praw0"), Buf("praw1")]

    def gen_prep():
        for wname, w_h, b_h, a_h, n_tiles, tile_cols, c_ws in (
                ("w1", wfc_h, bfcB_h, afc_h, NF, D, c_ws1),
                ("w2", wpr_h, bprB_h, apr_h, ND, FF, c_ws2)):
            nq = tile_cols // 1024
            for u in range(n_tiles * nq):
                t, q = u // nq, u % nq
                sl = pL[u % 3]
                dma("scalar", sl, w_h[128 * t:128 * t + 128,
                                      1024 * q:1024 * q + 1024], [], [bpL[u % 3]])
                if u == 0:
                    vop(lambda e, s, _sl=sl: e.tensor_reduce(
                        stc(c_wmax), _sl, AX.X, ALU.max, apply_absolute_value=True),
                        [bpL[u % 3]], [bST[c_wmax]])
                else:
                    vop(lambda e, s, _sl=sl: e.tensor_reduce(
                        stc(c_ptm), _sl, AX.X, ALU.max, apply_absolute_value=True),
                        [bpL[u % 3]], [bST[c_ptm]])
                    vop(lambda e, s: e.tensor_max(stc(c_wmax), stc(c_wmax),
                                                  stc(c_ptm)),
                        [bST[c_ptm], bST[c_wmax]], [bST[c_wmax]])
                yield
            tr(psum[4][0:1, 0:128], stc(c_wmax), ident[:],
               [bST[c_wmax], bident], [bPS[4]])
            vop(lambda e, s: e.reduce_max(ST[0:1, c_ptmp:c_ptmp + 1],
                                          psum[4][0:1, 0:128], AX.X),
                [bPS[4]], [bST[c_ptmp]])
            vop(lambda e, s: e.tensor_scalar_mul(
                ST[0:1, c_ptmp:c_ptmp + 1], ST[0:1, c_ptmp:c_ptmp + 1],
                1.0 / 127.0), [bST[c_ptmp]], [bST[c_ptmp]])
            mm(psum[4][:, 0:1], sones[:], ST[0:1, c_ptmp:c_ptmp + 1], True, True,
               [bsones, bST[c_ptmp]], [bPS[4]])
            vop(lambda e, s, _c=c_ws: e.tensor_copy(stc(_c), psum[4][:, 0:1]),
                [bPS[4]], [bST[c_ws]])
            vop(lambda e, s, _c=c_ws: e.reciprocal(stc(c_wsi), stc(_c)),
                [bST[c_ws]], [bST[c_wsi]])
            yield
            for q in range(nq):
                if wname == "w1":
                    if q == 0:
                        dma("scalar", aCh, a_h, [], [baCh])
                else:
                    dma("scalar", aCh, a_h[:, 1024 * q:1024 * q + 1024], [], [baCh])
                for t in range(n_tiles):
                    u = q * n_tiles + t
                    rs = praw[u % 2]
                    dma("scalar", rs, b_h[128 * t:128 * t + 128, :], [], [bpraw[u % 2]])
                    tr(psum[4][0:16, 0:128], rs, ident[:],
                       [bpraw[u % 2], bident], [bPS[4]])
                    vop(lambda e, s, _u=u: e.tensor_scalar_mul(
                        bTs[_u % 2], psum[4][0:16, 0:128], 2.0),
                        [bPS[4]], [bbT[u % 2]])
                    sl = pL[u % 3]
                    dma("scalar", sl, w_h[128 * t:128 * t + 128,
                                          1024 * q:1024 * q + 1024], [], [bpL[u % 3]])
                    vop(lambda e, s, _sl=sl: e.tensor_single_scalar(
                        pqa, _sl, stc(c_wsi), ALU.mult),
                        [bpL[u % 3], bST[c_wsi]], [bpqa])
                    vop(lambda e, s: e.tensor_scalar(
                        pqa, pqa, stc(c_cg), stc(c_cg), ALU.add, ALU.subtract),
                        [bpqa, bST[c_cg]], [bpqa])
                    vop(lambda e, s, _c=c_ws: e.tensor_single_scalar(
                        pqa, pqa, stc(_c), ALU.mult),
                        [bpqa, bST[c_ws]], [bpqa])
                    for cc in range(8):
                        tb = 5 + (cc % 2)
                        mm(psum[tb][:, 0:128], pqa[:, 128 * cc:128 * cc + 128],
                           ident[:], True, False,
                           [bpqa, bident], [bPS[tb]], is_tr=True)
                        mm(psum[tb][:, 0:128], aCh[:, 128 * cc:128 * cc + 128],
                           bTs[u % 2], False, True,
                           [baCh, bbT[u % 2]], [bPS[tb]])
                        sop(lambda e, s, _cc=cc, _tb=tb: e.activation(
                            pstage[:, 128 * _cc:128 * _cc + 128],
                            psum[_tb][:, 0:128], AF.Identity),
                            [bPS[tb]], [bpstage])
                    if wname == "w1":
                        dma("gpsimd", w1t_h[128 * t:128 * t + 128, :],
                            pstage[:], [bpstage], [bHBM["w1t"]])
                    else:
                        dst = bass.AP(w2t_h.tensor, 128 * t + 1024 * q * D,
                                      [[D, 128], [128 * D, 8], [1, 128]])
                        dma("gpsimd", dst, pstage[:], [bpstage], [bHBM["w2t"]])
                    yield

    P.barrier()

    def emit_ln(xslot, bslot):
        vop(lambda e, s, _x=xslot: e.reduce_sum(stc(c_sum), _x, AX.X),
            [bslot], [bST[c_sum]])
        sop(lambda e, s, _x=xslot: e.activation(psum[7][:, 0:512], _x[:, 0:512],
                                                AF.Square, accum_out=stc(c_sq)),
            [bslot], [bPS[7], bST[c_sq]])
        sop(lambda e, s, _x=xslot: e.activation(psum[7][:, 0:512], _x[:, 512:1024],
                                                AF.Square, accum_out=stc(c_tm)),
            [bslot], [bPS[7], bST[c_tm]])
        vop(lambda e, s: e.tensor_add(stc(c_sq), stc(c_sq), stc(c_tm)),
            [bST[c_sq], bST[c_tm]], [bST[c_sq]])
        vop(lambda e, s: e.tensor_scalar_mul(stc(c_m), stc(c_sum), 1.0 / D),
            [bST[c_sum]], [bST[c_m]])
        vop(lambda e, s: e.tensor_mul(stc(c_tmp), stc(c_m), stc(c_m)),
            [bST[c_m]], [bST[c_tmp]])
        vop(lambda e, s: e.scalar_tensor_tensor(
            stc(c_var), stc(c_sq), 1.0 / D, stc(c_tmp), ALU.mult, ALU.subtract),
            [bST[c_sq], bST[c_tmp]], [bST[c_var]])
        sop(lambda e, s: e.activation(stc(c_tm), stc(c_var), AF.Sqrt,
                                      bias=stc(c_eps)),
            [bST[c_var], bST[c_eps]], [bST[c_tm]])
        vop(lambda e, s: e.reciprocal(stc(c_r), stc(c_tm)),
            [bST[c_tm]], [bST[c_r]])
        vop(lambda e, s: e.scalar_tensor_tensor(
            stc(c_nmr), stc(c_m), -1.0, stc(c_r), ALU.mult, ALU.mult),
            [bST[c_m], bST[c_r]], [bST[c_nmr]])
        sop(lambda e, s, _x=xslot: e.activation(_x, _x, AF.Identity,
                                                bias=stc(c_nmr), scale=stc(c_r)),
            [bST[c_nmr], bST[c_r], bslot], [bslot])

    qT = AW[0:64, 0:4096]
    kT = AW[0:64, 4096:8192]
    bqT, bkT = Buf("qT"), Buf("kT")

    def gen_p1ab():
        wv_src = bass.AP(wv_h.tensor, 0, [[D, 128], [128 * D, ND], [1, D]])
        dma("sync", AW[:], wv_src, [], list(bAW))
        for p in range(NPAN):
            for tt in range(2):
                t = 2 * p + tt
                xs = A5[:, 1024 * (t % 2):1024 * (t % 2) + 1024]
                bxs = bA5[t % 2]
                dma("sync", xs, x_h[128 * t:128 * t + 128, :], [], [bxs])
                emit_ln(xs, bxs)
                dma("gpsimd", xa_h[128 * t:128 * t + 128, :], xs, [bxs],
                    [bHBM["xa"]])
                for c in range(ND):
                    tb = c % 2
                    tr(psum[tb][:, 0:128], xs[:, 128 * c:128 * c + 128], ident[:],
                       [bxs, bident], [bPS[tb]])
                    vop(lambda e, s, _c=c, _tt=tt, _tb=tb: e.tensor_copy(
                        A4[:, 256 * _c + 128 * _tt:256 * _c + 128 * _tt + 128],
                        psum[_tb][:, 0:128]),
                        [bPS[tb]], [bA4[(256 * c + 128 * tt) // 512]])
                for hh in range(2):
                    vb = 2 + hh
                    for c in range(ND):
                        mm(psum[vb][:, 0:512],
                           A4[:, 256 * c + 128 * tt:256 * c + 128 * tt + 128],
                           AW[:, 1024 * c + 512 * hh:1024 * c + 512 * hh + 512],
                           c == 0, c == ND - 1,
                           [bA4[(256 * c) // 512],
                            bAW[(1024 * c + 512 * hh) // 512]],
                           [bPS[vb]])
                    vop(lambda e, s, _t=t, _hh=hh, _vb=vb: e.tensor_copy(
                        A1[:, 1024 * _t + 512 * _hh:1024 * _t + 512 * _hh + 512],
                        psum[_vb][:, 0:512]),
                        [bPS[vb]], [bA1[(1024 * t + 512 * hh) // 512]])
                if tt == 1:
                    dma("gpsimd", yt_h[128 * p:128 * p + 128, :], A4[:],
                        [bA4[0], bA4[1], bA4[2], bA4[3]], [byt[p]])
                yield
        pnl = [A4, A5]
        bpnl = [[bA4[0], bA4[1], bA4[2], bA4[3]], [bA5[0], bA5[1]]]
        for p in range(NPAN):
            pb_ = pnl[p % 2]
            bb_ = bpnl[p % 2]
            dma("sync", pb_[:], yt_h[128 * p:128 * p + 128, :], [byt[p]], bb_)
            for qk in range(2):
                bank = 2 + qk
                wsb, bwsb = (swq, bswq) if qk == 0 else (swk, bswk)
                dst, bdst = (qT, bqT) if qk == 0 else (kT, bkT)
                for c in range(ND):
                    mm(psum[bank][0:64, 0:256], wsb[:, DQ * c:DQ * c + DQ],
                       pb_[:, 256 * c:256 * c + 256], c == 0, c == ND - 1,
                       [bwsb] + bb_, [bPS[bank]])
                aw_lo = (4096 * qk + 256 * p) // 512
                vop(lambda e, s, _p=p, _d=dst, _b=bank: e.tensor_copy(
                    _d[:, 256 * _p:256 * _p + 256], psum[_b][0:64, 0:256]),
                    [bPS[bank]], [bdst, bAW[aw_lo]])
            yield

    _SENT = object()
    gp, g1 = gen_prep(), gen_p1ab()
    ap_, a1_ = True, True
    while ap_ or a1_:
        for _ in range(3):
            if ap_:
                ap_ = next(gp, _SENT) is not _SENT
        if a1_:
            a1_ = next(g1, _SENT) is not _SENT

    P.barrier()

    Pbuf = A2[:, 0:4096]
    mask128 = A2[:, 4096:4224]
    dma("sync", mask128, cmask_h, [], [bmask])

    def emit_ln2p(xslot, bslot):
        vop(lambda e, s, _x=xslot: e.reduce_sum(stc(d_sum), _x, AX.X),
            [bslot], [bST[d_sum]])
        sop(lambda e, s, _x=xslot: e.activation(psum[6][:, 0:512], _x[:, 0:512],
                                                AF.Square, accum_out=stc(d_sq)),
            [bslot], [bPS[6], bST[d_sq]])
        sop(lambda e, s, _x=xslot: e.activation(psum[6][:, 0:512], _x[:, 512:1024],
                                                AF.Square, accum_out=stc(d_t1)),
            [bslot], [bPS[6], bST[d_t1]])
        vop(lambda e, s: e.tensor_add(stc(d_sq), stc(d_sq), stc(d_t1)),
            [bST[d_sq], bST[d_t1]], [bST[d_sq]])
        vop(lambda e, s: e.tensor_scalar_mul(stc(d_m), stc(d_sum), 1.0 / D),
            [bST[d_sum]], [bST[d_m]])
        vop(lambda e, s: e.tensor_mul(stc(d_t2), stc(d_m), stc(d_m)),
            [bST[d_m]], [bST[d_t2]])
        vop(lambda e, s: e.scalar_tensor_tensor(
            stc(d_var), stc(d_sq), 1.0 / D, stc(d_t2), ALU.mult, ALU.subtract),
            [bST[d_sq], bST[d_t2]], [bST[d_var]])
        sop(lambda e, s: e.activation(stc(d_t1), stc(d_var), AF.Ln,
                                      bias=stc(c_eps)),
            [bST[d_var], bST[c_eps]], [bST[d_t1]])
        sop(lambda e, s: e.activation(stc(d_r), stc(d_t1), AF.Exp,
                                      scale=np.float32(-0.5)),
            [bST[d_t1]], [bST[d_r]])
        vop(lambda e, s: e.scalar_tensor_tensor(
            stc(d_nmr), stc(d_m), -1.0, stc(d_r), ALU.mult, ALU.mult),
            [bST[d_m], bST[d_r]], [bST[d_nmr]])
        sop(lambda e, s, _x=xslot: e.activation(_x, _x, AF.Identity,
                                                bias=stc(d_nmr), scale=stc(d_r)),
            [bST[d_nmr], bST[d_r], bslot], [bslot])

    def s_phase(i):
        L = 128 * (i + 1)
        nkt = (L + 511) // 512
        xs = A5[:, 1024 * (i % 2):1024 * (i % 2) + 1024]
        dma("sync", xs, x_h[128 * i:128 * i + 128, :], [], [bA5[i % 2]])
        for kt in range(nkt):
            w = min(512, L - 512 * kt)
            sb_ = kt % 2
            mm(psum[sb_][:, 0:w], qT[:, 128 * i:128 * i + 128],
               kT[:, 512 * kt:512 * kt + w], True, True,
               [bqT, bkT], [bPS[sb_]])
            last = (kt == nkt - 1)
            if last:
                if w > 128:
                    vop(lambda e, s, _kt=kt, _sb=sb_, _w=w: e.tensor_copy(
                        Pbuf[:, 512 * _kt:512 * _kt + _w - 128],
                        psum[_sb][:, 0:_w - 128]),
                        [bPS[sb_]], [bA2s[kt]])
                vop(lambda e, s, _kt=kt, _sb=sb_, _w=w: e.tensor_add(
                    Pbuf[:, 512 * _kt + _w - 128:512 * _kt + _w],
                    psum[_sb][:, _w - 128:_w], mask128),
                    [bPS[sb_], bmask], [bA2s[kt]])
            else:
                vop(lambda e, s, _kt=kt, _sb=sb_: e.tensor_copy(
                    Pbuf[:, 512 * _kt:512 * _kt + 512], psum[_sb][:, 0:512]),
                    [bPS[sb_]], [bA2s[kt]])
            if kt == 0:
                vop(lambda e, s, _kt=kt, _w=w: e.reduce_max(
                    stc(c_max), Pbuf[:, 512 * _kt:512 * _kt + _w], AX.X),
                    [bA2s[kt]], [bST[c_max]])
            else:
                vop(lambda e, s, _kt=kt, _w=w: e.reduce_max(
                    stc(c_tm), Pbuf[:, 512 * _kt:512 * _kt + _w], AX.X),
                    [bA2s[kt]], [bST[c_tm]])
                vop(lambda e, s: e.tensor_max(stc(c_max), stc(c_max), stc(c_tm)),
                    [bST[c_max], bST[c_tm]], [bST[c_max]])
        vop(lambda e, s: e.tensor_scalar_mul(stc(c_nm8), stc(c_max), -QSCALE),
            [bST[c_max]], [bST[c_nm8]])
        for kt in range(nkt):
            w = min(512, L - 512 * kt)
            acc = c_es if kt == 0 else c_tm
            sop(lambda e, s, _kt=kt, _w=w, _a=acc: e.activation(
                Pbuf[:, 512 * _kt:512 * _kt + _w],
                Pbuf[:, 512 * _kt:512 * _kt + _w], AF.Exp,
                bias=stc(c_nm8), scale=stc(c_q8), accum_out=stc(_a)),
                [bST[c_nm8], bST[c_q8], bA2s[kt]], [bA2s[kt], bST[acc]])
            if kt > 0:
                vop(lambda e, s: e.tensor_add(stc(c_es), stc(c_es), stc(c_tm)),
                    [bST[c_es], bST[c_tm]], [bST[c_es]])
        vop(lambda e, s: e.reciprocal(stc(c_rs), stc(c_es)),
            [bST[c_es]], [bST[c_rs]])

    def pv_phase(i):
        ob = (4, 5)

        def ptrans(kc):
            tb = 2 + (kc % 2)
            tr(psum[tb][:, 0:128], Pbuf[:, 128 * kc:128 * kc + 128], ident[:],
               [bA2s[kc // 4], bident], [bPS[tb]])
            vop(lambda e, s, _kc=kc, _tb=tb: e.tensor_copy(
                A7[:, 128 * (_kc % 4):128 * (_kc % 4) + 128], psum[_tb][:, 0:128]),
                [bPS[tb]], [bA7[kc % 4]])

        ptrans(0)
        if i >= 1:
            ptrans(1)
        for kc in range(i + 1):
            if kc + 2 <= i:
                ptrans(kc + 2)
            for hh in range(2):
                mm(psum[ob[hh]][:, 0:512],
                   A7[:, 128 * (kc % 4):128 * (kc % 4) + 128],
                   A1[:, 1024 * kc + 512 * hh:1024 * kc + 512 * hh + 512],
                   kc == 0, kc == i,
                   [bA7[kc % 4], bA1[(1024 * kc + 512 * hh) // 512]],
                   [bPS[ob[hh]]])
        xs = A5[:, 1024 * (i % 2):1024 * (i % 2) + 1024]
        for hh in range(2):
            vop(lambda e, s, _hh=hh, _b=ob[hh], _xs=xs:
                e.scalar_tensor_tensor(
                    _xs[:, 512 * _hh:512 * _hh + 512], psum[_b][:, 0:512],
                    stc(c_rs), _xs[:, 512 * _hh:512 * _hh + 512],
                    ALU.mult, ALU.add),
                [bPS[ob[hh]], bST[c_rs], bA5[i % 2]], [bA5[i % 2]])
        dma("gpsimd", xa_h[128 * i:128 * i + 128, :], xs, [bA5[i % 2]],
            [bHBM["xa"]])
        emit_ln2p(xs, bA5[i % 2])
        dma("gpsimd", y2_h[128 * i:128 * i + 128, :], xs, [bA5[i % 2]],
            [by2[i]])

    for i in range(NT):
        s_phase(i)
        pv_phase(i)

    P.barrier()

    sbpr = AW[0:1, 5120:6144]
    bsbpr = Buf("sbpr")
    bprT = A2[:, 4224:5248]
    dma("sync", sbpr, bass.AP(bpr_h.tensor, 0, [[D, 1], [1, D]]), [], [bsbpr])
    for hh in range(2):
        mm(psum[4 + hh][:, 0:512], sones[:], sbpr[:, 512 * hh:512 * hh + 512],
           True, True, [bsones, bsbpr], [bPS[4 + hh]])
        vop(lambda e, s, _hh=hh: e.tensor_copy(
            bprT[:, 512 * _hh:512 * _hh + 512], psum[4 + _hh][:, 0:512]),
            [bPS[4 + hh]], [bbprT])

    w1s = [A4[:, 0:1024], A4[:, 1024:2048]]
    bw1s = [Buf("w1s0"), Buf("w1s1")]
    hbuf = AW[:, 0:4096]
    bhbuf = Buf("hb")
    NPAN_T = T // 512

    def gen_p3():
        dma("sync", A5[:, 0:1024], y2_h[0:128, :], [by2[0]], [bA5[0]])
        for t in range(NT):
            xs = A5[:, 1024 * (t % 2):1024 * (t % 2) + 1024]
            if t + 1 < NT:
                dma("sync",
                    A5[:, 1024 * ((t + 1) % 2):1024 * ((t + 1) % 2) + 1024],
                    y2_h[128 * (t + 1):128 * (t + 1) + 128, :],
                    [by2[t + 1]], [bA5[(t + 1) % 2]])
            for c in range(ND):
                tb = c % 2
                tr(psum[tb][:, 0:128], xs[:, 128 * c:128 * c + 128], ident[:],
                   [bA5[t % 2], bident], [bPS[tb]])
                vop(lambda e, s, _c=c, _t=t, _tb=tb: e.tensor_copy(
                    A1[:, 4096 * _c + 128 * _t:4096 * _c + 128 * _t + 128],
                    psum[_tb][:, 0:128]),
                    [bPS[tb]], [bA1[(4096 * c + 128 * t) // 512]])
            if t % 4 == 3:
                yield

    def gen_fc():
        for fc in range(NF):
            dma("sync", w1s[fc % 2][:], w1t_h[128 * fc:128 * fc + 128, :],
                [bHBM["w1t"]], [bw1s[fc % 2]])
            for p in range(NPAN_T):
                hb = 2 + (fc * NPAN_T + p) % 2
                for c in range(ND):
                    mm(psum[hb][:, 0:512], w1s[fc % 2][:, 128 * c:128 * c + 128],
                       A1[:, 4096 * c + 512 * p:4096 * c + 512 * p + 512],
                       c == 0, c == ND - 1,
                       [bw1s[fc % 2], bA1[(4096 * c + 512 * p) // 512]],
                       [bPS[hb]])
                sop(lambda e, s, _fc=fc, _p=p, _hb=hb: e.activation(
                    hbuf[:, 512 * _p:512 * _p + 512], psum[_hb][:, 0:512],
                    AF.Gelu, bias=sbfc[:, _fc:_fc + 1]),
                    [bPS[hb], bsbfc], [bhbuf])
                if fc == 0:
                    yield
            dma("gpsimd", hT_h[128 * fc:128 * fc + 128, 0:T],
                hbuf[:, 0:T], [bhbuf], [bhT[fc]])

    g3, gf = gen_p3(), gen_fc()
    a3_, af_ = True, True
    while a3_ or af_:
        if a3_:
            a3_ = next(g3, _SENT) is not _SENT
        if af_:
            af_ = next(gf, _SENT) is not _SENT


    w2s = [A2[:, 1024 * j:1024 * j + 1024] for j in range(4)] + \
          [A4[:, 0:1024], A4[:, 1024:2048]]
    bw2s = [Buf(f"w2s{j}") for j in range(6)]
    hts = [AW[:, 4096 + 256 * j:4096 + 256 * j + 256] for j in range(4)]
    bhts = [Buf(f"hts{j}") for j in range(4)]

    for sp in range(NSUP):
        pb = 4 * ((sp + 1) % 2)
        for tt in range(2):
            t = 2 * sp + tt
            xs = A5[:, 1024 * tt:1024 * tt + 1024]
            dma("gpsimd", xs, xa_h[128 * t:128 * t + 128, :], [], [bA5[tt]],
                extra=[bHBM["xa"].writer])
        for fc in range(NF):
            j = fc % 6
            dma("sync" if fc % 2 == 0 else "scalar",
                w2s[j][:], w2t_h[128 * fc:128 * fc + 128, :],
                [], [bw2s[j]] + ([bw1s[j - 4]] if j >= 4 else []),
                extra=[bHBM["w2t"].writer])
            dma("sync", hts[fc % 4][:],
                hT_h[128 * fc:128 * fc + 128, 256 * sp:256 * sp + 256],
                [bhT[fc]], [bhts[fc % 4]])
            for tt in range(2):
                for hh in range(2):
                    mm(psum[pb + 2 * tt + hh][:, 0:512],
                       hts[fc % 4][:, 128 * tt:128 * tt + 128],
                       w2s[j][:, 512 * hh:512 * hh + 512],
                       fc == 0, fc == NF - 1,
                       [bhts[fc % 4], bw2s[j]], [bPS[pb + 2 * tt + hh]])
        for tt in range(2):
            t = 2 * sp + tt
            xs = A5[:, 1024 * tt:1024 * tt + 1024]
            for hh in range(2):
                bk = pb + 2 * tt + hh
                vop(lambda e, s, _bk=bk, _xs=xs, _hh=hh: e.tensor_add(
                    _xs[:, 512 * _hh:512 * _hh + 512], psum[_bk][:, 0:512],
                    _xs[:, 512 * _hh:512 * _hh + 512]),
                    [bPS[bk], bA5[tt]], [bA5[tt]])
                vop(lambda e, s, _bk=bk, _xs=xs, _hh=hh: e.tensor_add(
                    _xs[:, 512 * _hh:512 * _hh + 512],
                    _xs[:, 512 * _hh:512 * _hh + 512],
                    bprT[:, 512 * _hh:512 * _hh + 512]),
                    [bA5[tt], bbprT], [bA5[tt]])
            dma("gpsimd", y_h[128 * t:128 * t + 128, :], xs, [bA5[tt]],
                [bHBM["y"]])

    final_st = [(s, P.ctr[s]) for s in Prog.DPOOL["gpsimd"] if P.ctr[s] > 0]

    FIN = P.finalize(extra_needed=final_st)
    final_st = [(s, P.remap[s][v]) for s, v in final_st]

    def replay(eng, sems, prog):
        for build, waits, sem, amt in prog:
            for s, v in waits:
                eng.wait_ge(sems[s], v)
            if build is not None:
                ins = build(eng, sems)
                if sem is not None:
                    ins.then_inc(sems[sem], amt)

    with es:
        sems = {}
        for s in P.ctr:
            sems[s] = es.enter_context(nc.semaphore(s))
        with nc.Block() as block:

            @block.tensor
            def _(eng):
                replay(eng, sems, FIN["tensor"])

            @block.vector
            def _(eng):
                replay(eng, sems, FIN["vector"])

            @block.scalar
            def _(eng):
                replay(eng, sems, FIN["scalar"])

            @block.sync
            def _(eng):
                replay(eng, sems, FIN["sync"])

            @block.gpsimd
            def _(eng):
                replay(eng, sems, FIN["gpsimd"])
                for s, v in final_st:
                    eng.wait_ge(sems[s], v)

    return nc


def make_masks():
    col = np.arange(128, dtype=np.int64)[None, :]
    row = np.arange(128, dtype=np.int64)[:, None]
    return np.where(col <= row, np.float32(0.0), np.float32(NEG)).astype(np.float32)


def kernel(L0_attn_X, L0_attn_Wq, L0_attn_Wk, L0_attn_Wv,
           L0_ffn_W_fc, L0_ffn_b_fc, L0_ffn_A_fc, L0_ffn_B_fc,
           L0_ffn_W_proj, L0_ffn_b_proj, L0_ffn_A_proj, L0_ffn_B_proj):
    X = np.ascontiguousarray(np.asarray(L0_attn_X, dtype=np.float32))
    B, T, _ = X.shape
    assert B == 8 and T % 512 == 0

    nc = build_program(T)
    common = {
        "wq": np.ascontiguousarray(L0_attn_Wq, dtype=np.float32),
        "wk": np.ascontiguousarray(L0_attn_Wk, dtype=np.float32),
        "wv": np.ascontiguousarray(L0_attn_Wv, dtype=np.float32),
        "wfc": np.ascontiguousarray(L0_ffn_W_fc, dtype=np.float32),
        "bfc": np.ascontiguousarray(L0_ffn_b_fc, dtype=np.float32),
        "afc": np.ascontiguousarray(L0_ffn_A_fc, dtype=np.float32),
        "bfcB": np.ascontiguousarray(L0_ffn_B_fc, dtype=np.float32),
        "wpr": np.ascontiguousarray(L0_ffn_W_proj, dtype=np.float32),
        "bpr": np.ascontiguousarray(L0_ffn_b_proj, dtype=np.float32),
        "apr": np.ascontiguousarray(L0_ffn_A_proj, dtype=np.float32),
        "bprB": np.ascontiguousarray(L0_ffn_B_proj, dtype=np.float32),
        "ident": np.eye(128, dtype=np.float32),
        "cmask": make_masks(),
    }
    in_maps = [dict(common, x=np.ascontiguousarray(X[i])) for i in range(8)]
    res = run_bass_kernel_spmd(nc, in_maps, core_ids=list(range(8)))
    out = np.stack([np.asarray(r["y"], dtype=np.float32) for r in res.results])
    return out


# version 26 (final)
# speedup vs baseline: 1.0138x; 1.0030x over previous
import contextlib
import numpy as np
import concourse.bass as bass
import concourse.mybir as mybir
from concourse.bass_utils import run_bass_kernel_spmd

F32 = mybir.dt.float32
AF = mybir.ActivationFunctionType
ALU = mybir.AluOpType
AX = mybir.AxisListType

D = 1024
DQ = 64
FF = 4096
ND = D // 128
NF = FF // 128
EPS = 1e-5
NEG = -1.0e30
QSCALE = 0.125


class Buf:
    __slots__ = ("writer", "readers", "name")

    def __init__(self, name=""):
        self.writer = None
        self.readers = []
        self.name = name


class Prog:
    ENGINES = ("tensor", "vector", "scalar", "sync", "gpsimd")
    ESEM = {"tensor": "pe", "vector": "ve", "scalar": "se"}
    DPOOL = {"sync": [f"ld{i}" for i in range(12)],
             "gpsimd": [f"st{i}" for i in range(4)],
             "scalar": [f"sd{i}" for i in range(6)]}

    def __init__(self):
        self.sched = {e: [] for e in self.ENGINES}
        self.ctr = {s: 0 for s in ("pe", "ve", "se")}
        for pool in self.DPOOL.values():
            for s in pool:
                self.ctr[s] = 0
        self.dma_n = {e: 0 for e in self.DPOOL}
        self.wm = {e: {} for e in self.ENGINES}

    def _waits(self, eng, tokens):
        merged = {}
        for t in tokens:
            if t is None:
                continue
            s, v = t
            if eng == "tensor" and s == "pe":
                continue
            if merged.get(s, -1) < v:
                merged[s] = v
        out = []
        for s, v in merged.items():
            if self.wm[eng].get(s, 0) >= v:
                continue
            out.append((s, v))
            self.wm[eng][s] = v
        return out

    def emit(self, eng, build, reads=(), writes=(), dma=False, extra=()):
        toks = []
        for b in reads:
            toks.append(b.writer)
        for b in writes:
            toks.append(b.writer)
            toks.extend(b.readers)
        toks.extend(extra)

        if dma:
            pool = self.DPOOL[eng]
            sem = pool[self.dma_n[eng] % len(pool)]
            self.dma_n[eng] += 1
            if self.ctr[sem] > 0:
                toks.append((sem, self.ctr[sem]))
            amt = 16
        else:
            sem, amt = self.ESEM[eng], 1
        waits = self._waits(eng, toks)
        self.ctr[sem] += amt
        token = (sem, self.ctr[sem])

        def run(engobj, sems, _waits=waits, _build=build, _sem=sem, _amt=amt):
            for s, v in _waits:
                engobj.wait_ge(sems[s], v)
            ins = _build(engobj, sems)
            ins.then_inc(sems[_sem], _amt)

        self.sched[eng].append(run)

        for b in reads:
            b.readers.append(token)
        for b in writes:
            b.writer = token
            b.readers = []
        return token

    def barrier(self):
        snap = dict(self.ctr)
        for eng in self.ENGINES:
            waits = []
            for s, v in snap.items():
                if v > 0 and self.wm[eng].get(s, 0) < v:
                    waits.append((s, v))
                    self.wm[eng][s] = v
            if not waits:
                continue

            def run(engobj, sems, _waits=waits):
                for s, v in _waits:
                    engobj.wait_ge(sems[s], v)

            self.sched[eng].append(run)


def build_program(T):
    NT = T // 128
    NPAN = T // 256
    NSUP = NT // 2

    nc = bass.Bass("TRN2", target_bir_lowering=False)

    x_h = nc.dram_tensor("x", [T, D], F32, kind="ExternalInput").ap()
    wq_h = nc.dram_tensor("wq", [D, DQ], F32, kind="ExternalInput").ap()
    wk_h = nc.dram_tensor("wk", [D, DQ], F32, kind="ExternalInput").ap()
    wv_h = nc.dram_tensor("wv", [D, D], F32, kind="ExternalInput").ap()
    wfc_h = nc.dram_tensor("wfc", [FF, D], F32, kind="ExternalInput").ap()
    bfc_h = nc.dram_tensor("bfc", [FF], F32, kind="ExternalInput").ap()
    afc_h = nc.dram_tensor("afc", [16, D], F32, kind="ExternalInput").ap()
    bfcB_h = nc.dram_tensor("bfcB", [FF, 16], F32, kind="ExternalInput").ap()
    wpr_h = nc.dram_tensor("wpr", [D, FF], F32, kind="ExternalInput").ap()
    bpr_h = nc.dram_tensor("bpr", [D], F32, kind="ExternalInput").ap()
    apr_h = nc.dram_tensor("apr", [16, FF], F32, kind="ExternalInput").ap()
    bprB_h = nc.dram_tensor("bprB", [D, 16], F32, kind="ExternalInput").ap()
    ident_h = nc.dram_tensor("ident", [128, 128], F32, kind="ExternalInput").ap()
    cmask_h = nc.dram_tensor("cmask", [128, 128], F32, kind="ExternalInput").ap()

    y_h = nc.dram_tensor("y", [T, D], F32, kind="ExternalOutput").ap()
    y2_h = nc.dram_tensor("y2", [T, D], F32, kind="Internal").ap()
    xa_h = nc.dram_tensor("xa", [T, D], F32, kind="Internal").ap()
    hT_h = nc.dram_tensor("hT", [FF, T], F32, kind="Internal").ap()
    w1t_h = nc.dram_tensor("w1t", [NF * 128, D], F32, kind="Internal").ap()
    w2t_h = nc.dram_tensor("w2t", [NF * 128, D], F32, kind="Internal").ap()
    yt_h = nc.dram_tensor("yt", [T // 256 * 128, 2048], F32, kind="Internal").ap()

    es = contextlib.ExitStack()
    sb = lambda name, shape: es.enter_context(nc.sbuf_tensor(name, shape, F32))

    A1 = sb("A1", [128, 32768])
    AW = sb("AW", [128, 8192])
    A2 = sb("A2", [128, 6272])
    A4 = sb("A4", [128, 2048])
    A5 = sb("A5", [128, 2048])
    A7 = sb("A7", [128, 512])
    ident = sb("identt", [128, 128])
    swq = sb("swq", [128, 8 * DQ])
    swk = sb("swk", [128, 8 * DQ])
    sbfc = sb("sbfc", [128, 32])
    sones = sb("sones", [1, 128])
    ST = sb("stats", [128, 32])

    c_sum, c_sq, c_m, c_var, c_r, c_nmr = 0, 1, 2, 3, 4, 5
    c_max, c_tm, c_es, c_rs, c_nm8 = 6, 7, 8, 9, 10
    c_wmax, c_ws1, c_ws2, c_tmp = 11, 12, 13, 14
    c_eps, c_q8 = 15, 16
    c_one, c_cg, c_wsi = 17, 18, 19
    c_ptm, c_ptmp = 20, 21
    d_sum, d_sq, d_m, d_var, d_r, d_nmr, d_t1, d_t2 = 24, 25, 26, 27, 28, 29, 30, 31
    C32 = 12582912.0
    C64 = 6755399441055744.0
    E30 = 2.0 ** -30

    def stc(i):
        return ST[:, i:i + 1]

    psum = [es.enter_context(nc.psum_tensor(f"ps{i}", [128, 512], F32))
            for i in range(8)]

    P = Prog()

    bA1 = [Buf(f"A1_{i}") for i in range(64)]
    bAW = [Buf(f"AW_{i}") for i in range(16)]
    bA2s = [Buf(f"A2P_{i}") for i in range(8)]
    bmask = Buf("mask")
    bbprT = Buf("bprT")
    bA4 = [Buf(f"A4_{i}") for i in range(4)]
    bA5 = [Buf("A5_0"), Buf("A5_1")]
    bA7 = [Buf(f"A7_{i}") for i in range(4)]
    bPS = [Buf(f"ps{i}") for i in range(8)]
    bident = Buf("ident")
    bswq, bswk = Buf("swq"), Buf("swk")
    bsbfc = Buf("sbfc")
    bsones = Buf("sones")
    bST = [Buf(f"st{i}") for i in range(32)]
    bHBM = {k: Buf(k) for k in ("xa", "hT", "w1t", "w2t", "y")}
    bhT = [Buf(f"hT{i}") for i in range(NF)]
    byt = [Buf(f"yt{i}") for i in range(T // 256)]
    by2 = [Buf(f"y2_{i}") for i in range(NT)]

    def dma(eng, dst, src, reads, writes, extra=()):
        return P.emit(eng, lambda e, s, _d=dst, _s=src: e.dma_start(out=_d, in_=_s),
                      reads=reads, writes=writes, dma=True, extra=extra)

    def mm(out, lhsT, rhs, start, stop, reads, writes, extra=(), is_tr=None):
        return P.emit("tensor",
                      lambda e, s, _o=out, _l=lhsT, _r=rhs, _st=start, _sp=stop,
                      _tr=is_tr:
                      e.matmul(_o, _l, _r, start=_st, stop=_sp, is_transpose=_tr),
                      reads=reads, writes=writes, extra=extra)

    def tr(out, in_, idn, reads, writes, extra=()):
        return P.emit("tensor",
                      lambda e, s, _o=out, _i=in_, _d=idn: e.transpose(_o, _i, _d),
                      reads=reads, writes=writes, extra=extra)

    def vop(fn, reads, writes, extra=()):
        return P.emit("vector", fn, reads=reads, writes=writes, extra=extra)

    def sop(fn, reads, writes, extra=()):
        return P.emit("scalar", fn, reads=reads, writes=writes, extra=extra)

    dma("sync", ident[:], ident_h, [], [bident])
    swq_src = bass.AP(wq_h.tensor, 0, [[DQ, 128], [128 * DQ, ND], [1, DQ]])
    swk_src = bass.AP(wk_h.tensor, 0, [[DQ, 128], [128 * DQ, ND], [1, DQ]])
    dma("sync", swq[:], swq_src, [], [bswq])
    dma("sync", swk[:], swk_src, [], [bswk])
    P.emit("vector", lambda e, s: e.memset(sones[:], 1.0), [], [bsones])
    P.emit("vector", lambda e, s: e.memset(ST[:, 15:16], EPS), [], [bST[15]])
    P.emit("vector", lambda e, s: e.memset(ST[:, 16:17], QSCALE), [], [bST[16]])
    P.emit("vector", lambda e, s: e.memset(stc(c_one), 1.0), [], [bST[c_one]])
    P.emit("vector", lambda e, s: e.memset(stc(c_tmp), E30), [], [bST[c_tmp]])
    P.emit("vector", lambda e, s: e.tensor_add(stc(c_tmp), stc(c_tmp), stc(c_one)),
           [bST[c_one], bST[c_tmp]], [bST[c_tmp]])
    P.emit("vector", lambda e, s: e.tensor_sub(stc(c_tmp), stc(c_tmp), stc(c_one)),
           [bST[c_one], bST[c_tmp]], [bST[c_tmp]])
    P.emit("vector", lambda e, s: e.tensor_scalar_mul(stc(c_tmp), stc(c_tmp), 2.0 ** 30),
           [bST[c_tmp]], [bST[c_tmp]])
    P.emit("vector", lambda e, s: e.memset(stc(c_cg), C32), [], [bST[c_cg]])
    P.emit("vector", lambda e, s: e.scalar_tensor_tensor(
        stc(c_cg), stc(c_tmp), C64, stc(c_cg), ALU.mult, ALU.add),
        [bST[c_tmp], bST[c_cg]], [bST[c_cg]])

    bfc_raw = A4[0:32, 640:768]
    dma("sync", bfc_raw, bass.AP(bfc_h.tensor, 0, [[128, 32], [1, 128]]),
        [], [bA4[1]])
    tr(psum[4][0:128, 0:32], bfc_raw, ident[0:32, 0:32], [bA4[1], bident], [bPS[4]])
    vop(lambda e, s: e.tensor_copy(sbfc[:], psum[4][0:128, 0:32]), [bPS[4]], [bsbfc])

    pL = [A2[:, 0:1024], A2[:, 1024:2048], A2[:, 5248:6272]]
    bpL = [Buf("pL0"), Buf("pL1"), Buf("pL2")]
    pqa = A2[:, 2048:3072]
    bpqa = Buf("pqa")
    pstage = A2[:, 3072:4096]
    bpstage = Buf("pstage")
    aCh = A2[0:16, 4096:5120]
    baCh = Buf("aCh")
    bTs = [A7[0:16, 0:128], A7[0:16, 128:256]]
    bbT = [Buf("bT0"), Buf("bT1")]
    praw = [A7[:, 256:272], A7[:, 272:288]]
    bpraw = [Buf("praw0"), Buf("praw1")]

    def gen_prep():
        for wname, w_h, b_h, a_h, n_tiles, tile_cols, c_ws in (
                ("w1", wfc_h, bfcB_h, afc_h, NF, D, c_ws1),
                ("w2", wpr_h, bprB_h, apr_h, ND, FF, c_ws2)):
            nq = tile_cols // 1024
            for u in range(n_tiles * nq):
                t, q = u // nq, u % nq
                sl = pL[u % 3]
                dma("scalar", sl, w_h[128 * t:128 * t + 128,
                                      1024 * q:1024 * q + 1024], [], [bpL[u % 3]])
                if u == 0:
                    vop(lambda e, s, _sl=sl: e.tensor_reduce(
                        stc(c_wmax), _sl, AX.X, ALU.max, apply_absolute_value=True),
                        [bpL[u % 3]], [bST[c_wmax]])
                else:
                    vop(lambda e, s, _sl=sl: e.tensor_reduce(
                        stc(c_ptm), _sl, AX.X, ALU.max, apply_absolute_value=True),
                        [bpL[u % 3]], [bST[c_ptm]])
                    vop(lambda e, s: e.tensor_max(stc(c_wmax), stc(c_wmax),
                                                  stc(c_ptm)),
                        [bST[c_ptm], bST[c_wmax]], [bST[c_wmax]])
                yield
            tr(psum[4][0:1, 0:128], stc(c_wmax), ident[:],
               [bST[c_wmax], bident], [bPS[4]])
            vop(lambda e, s: e.reduce_max(ST[0:1, c_ptmp:c_ptmp + 1],
                                          psum[4][0:1, 0:128], AX.X),
                [bPS[4]], [bST[c_ptmp]])
            vop(lambda e, s: e.tensor_scalar_mul(
                ST[0:1, c_ptmp:c_ptmp + 1], ST[0:1, c_ptmp:c_ptmp + 1],
                1.0 / 127.0), [bST[c_ptmp]], [bST[c_ptmp]])
            mm(psum[4][:, 0:1], sones[:], ST[0:1, c_ptmp:c_ptmp + 1], True, True,
               [bsones, bST[c_ptmp]], [bPS[4]])
            vop(lambda e, s, _c=c_ws: e.tensor_copy(stc(_c), psum[4][:, 0:1]),
                [bPS[4]], [bST[c_ws]])
            vop(lambda e, s, _c=c_ws: e.reciprocal(stc(c_wsi), stc(_c)),
                [bST[c_ws]], [bST[c_wsi]])
            yield
            for q in range(nq):
                if wname == "w1":
                    if q == 0:
                        dma("scalar", aCh, a_h, [], [baCh])
                else:
                    dma("scalar", aCh, a_h[:, 1024 * q:1024 * q + 1024], [], [baCh])
                for t in range(n_tiles):
                    u = q * n_tiles + t
                    rs = praw[u % 2]
                    dma("scalar", rs, b_h[128 * t:128 * t + 128, :], [], [bpraw[u % 2]])
                    tr(psum[4][0:16, 0:128], rs, ident[:],
                       [bpraw[u % 2], bident], [bPS[4]])
                    vop(lambda e, s, _u=u: e.tensor_scalar_mul(
                        bTs[_u % 2], psum[4][0:16, 0:128], 2.0),
                        [bPS[4]], [bbT[u % 2]])
                    sl = pL[u % 3]
                    dma("scalar", sl, w_h[128 * t:128 * t + 128,
                                          1024 * q:1024 * q + 1024], [], [bpL[u % 3]])
                    vop(lambda e, s, _sl=sl: e.tensor_single_scalar(
                        pqa, _sl, stc(c_wsi), ALU.mult),
                        [bpL[u % 3], bST[c_wsi]], [bpqa])
                    vop(lambda e, s: e.tensor_scalar(
                        pqa, pqa, stc(c_cg), stc(c_cg), ALU.add, ALU.subtract),
                        [bpqa, bST[c_cg]], [bpqa])
                    vop(lambda e, s, _c=c_ws: e.tensor_single_scalar(
                        pqa, pqa, stc(_c), ALU.mult),
                        [bpqa, bST[c_ws]], [bpqa])
                    for cc in range(8):
                        tb = 5 + (cc % 2)
                        mm(psum[tb][:, 0:128], pqa[:, 128 * cc:128 * cc + 128],
                           ident[:], True, False,
                           [bpqa, bident], [bPS[tb]], is_tr=True)
                        mm(psum[tb][:, 0:128], aCh[:, 128 * cc:128 * cc + 128],
                           bTs[u % 2], False, True,
                           [baCh, bbT[u % 2]], [bPS[tb]])
                        sop(lambda e, s, _cc=cc, _tb=tb: e.activation(
                            pstage[:, 128 * _cc:128 * _cc + 128],
                            psum[_tb][:, 0:128], AF.Identity),
                            [bPS[tb]], [bpstage])
                    if wname == "w1":
                        dma("gpsimd", w1t_h[128 * t:128 * t + 128, :],
                            pstage[:], [bpstage], [bHBM["w1t"]])
                    else:
                        dst = bass.AP(w2t_h.tensor, 128 * t + 1024 * q * D,
                                      [[D, 128], [128 * D, 8], [1, 128]])
                        dma("gpsimd", dst, pstage[:], [bpstage], [bHBM["w2t"]])
                    yield

    P.barrier()

    def emit_ln(xslot, bslot):
        vop(lambda e, s, _x=xslot: e.reduce_sum(stc(c_sum), _x, AX.X),
            [bslot], [bST[c_sum]])
        sop(lambda e, s, _x=xslot: e.activation(psum[7][:, 0:512], _x[:, 0:512],
                                                AF.Square, accum_out=stc(c_sq)),
            [bslot], [bPS[7], bST[c_sq]])
        sop(lambda e, s, _x=xslot: e.activation(psum[7][:, 0:512], _x[:, 512:1024],
                                                AF.Square, accum_out=stc(c_tm)),
            [bslot], [bPS[7], bST[c_tm]])
        vop(lambda e, s: e.tensor_add(stc(c_sq), stc(c_sq), stc(c_tm)),
            [bST[c_sq], bST[c_tm]], [bST[c_sq]])
        vop(lambda e, s: e.tensor_scalar_mul(stc(c_m), stc(c_sum), 1.0 / D),
            [bST[c_sum]], [bST[c_m]])
        vop(lambda e, s: e.tensor_mul(stc(c_tmp), stc(c_m), stc(c_m)),
            [bST[c_m]], [bST[c_tmp]])
        vop(lambda e, s: e.scalar_tensor_tensor(
            stc(c_var), stc(c_sq), 1.0 / D, stc(c_tmp), ALU.mult, ALU.subtract),
            [bST[c_sq], bST[c_tmp]], [bST[c_var]])
        sop(lambda e, s: e.activation(stc(c_tm), stc(c_var), AF.Sqrt,
                                      bias=stc(c_eps)),
            [bST[c_var], bST[c_eps]], [bST[c_tm]])
        vop(lambda e, s: e.reciprocal(stc(c_r), stc(c_tm)),
            [bST[c_tm]], [bST[c_r]])
        vop(lambda e, s: e.scalar_tensor_tensor(
            stc(c_nmr), stc(c_m), -1.0, stc(c_r), ALU.mult, ALU.mult),
            [bST[c_m], bST[c_r]], [bST[c_nmr]])
        sop(lambda e, s, _x=xslot: e.activation(_x, _x, AF.Identity,
                                                bias=stc(c_nmr), scale=stc(c_r)),
            [bST[c_nmr], bST[c_r], bslot], [bslot])

    qT = AW[0:64, 0:4096]
    kT = AW[0:64, 4096:8192]
    bqT, bkT = Buf("qT"), Buf("kT")

    def gen_p1ab():
        wv_src = bass.AP(wv_h.tensor, 0, [[D, 128], [128 * D, ND], [1, D]])
        dma("sync", AW[:], wv_src, [], list(bAW))
        for p in range(NPAN):
            for tt in range(2):
                t = 2 * p + tt
                xs = A5[:, 1024 * (t % 2):1024 * (t % 2) + 1024]
                bxs = bA5[t % 2]
                dma("sync", xs, x_h[128 * t:128 * t + 128, :], [], [bxs])
                emit_ln(xs, bxs)
                dma("gpsimd", xa_h[128 * t:128 * t + 128, :], xs, [bxs],
                    [bHBM["xa"]])
                for c in range(ND):
                    tb = c % 2
                    tr(psum[tb][:, 0:128], xs[:, 128 * c:128 * c + 128], ident[:],
                       [bxs, bident], [bPS[tb]])
                    vop(lambda e, s, _c=c, _tt=tt, _tb=tb: e.tensor_copy(
                        A4[:, 256 * _c + 128 * _tt:256 * _c + 128 * _tt + 128],
                        psum[_tb][:, 0:128]),
                        [bPS[tb]], [bA4[(256 * c + 128 * tt) // 512]])
                for hh in range(2):
                    vb = 2 + hh
                    for c in range(ND):
                        mm(psum[vb][:, 0:512],
                           A4[:, 256 * c + 128 * tt:256 * c + 128 * tt + 128],
                           AW[:, 1024 * c + 512 * hh:1024 * c + 512 * hh + 512],
                           c == 0, c == ND - 1,
                           [bA4[(256 * c) // 512],
                            bAW[(1024 * c + 512 * hh) // 512]],
                           [bPS[vb]])
                    vop(lambda e, s, _t=t, _hh=hh, _vb=vb: e.tensor_copy(
                        A1[:, 1024 * _t + 512 * _hh:1024 * _t + 512 * _hh + 512],
                        psum[_vb][:, 0:512]),
                        [bPS[vb]], [bA1[(1024 * t + 512 * hh) // 512]])
                if tt == 1:
                    dma("gpsimd", yt_h[128 * p:128 * p + 128, :], A4[:],
                        [bA4[0], bA4[1], bA4[2], bA4[3]], [byt[p]])
                yield
        pnl = [A4, A5]
        bpnl = [[bA4[0], bA4[1], bA4[2], bA4[3]], [bA5[0], bA5[1]]]
        for p in range(NPAN):
            pb_ = pnl[p % 2]
            bb_ = bpnl[p % 2]
            dma("sync", pb_[:], yt_h[128 * p:128 * p + 128, :], [byt[p]], bb_)
            for qk in range(2):
                bank = 2 + qk
                wsb, bwsb = (swq, bswq) if qk == 0 else (swk, bswk)
                dst, bdst = (qT, bqT) if qk == 0 else (kT, bkT)
                for c in range(ND):
                    mm(psum[bank][0:64, 0:256], wsb[:, DQ * c:DQ * c + DQ],
                       pb_[:, 256 * c:256 * c + 256], c == 0, c == ND - 1,
                       [bwsb] + bb_, [bPS[bank]])
                aw_lo = (4096 * qk + 256 * p) // 512
                vop(lambda e, s, _p=p, _d=dst, _b=bank: e.tensor_copy(
                    _d[:, 256 * _p:256 * _p + 256], psum[_b][0:64, 0:256]),
                    [bPS[bank]], [bdst, bAW[aw_lo]])
            yield

    _SENT = object()
    gp, g1 = gen_prep(), gen_p1ab()
    ap_, a1_ = True, True
    while ap_ or a1_:
        for _ in range(3):
            if ap_:
                ap_ = next(gp, _SENT) is not _SENT
        if a1_:
            a1_ = next(g1, _SENT) is not _SENT

    P.barrier()

    Pbuf = A2[:, 0:4096]
    mask128 = A2[:, 4096:4224]
    dma("sync", mask128, cmask_h, [], [bmask])

    def emit_ln2p(xslot, bslot):
        vop(lambda e, s, _x=xslot: e.reduce_sum(stc(d_sum), _x, AX.X),
            [bslot], [bST[d_sum]])
        sop(lambda e, s, _x=xslot: e.activation(psum[6][:, 0:512], _x[:, 0:512],
                                                AF.Square, accum_out=stc(d_sq)),
            [bslot], [bPS[6], bST[d_sq]])
        sop(lambda e, s, _x=xslot: e.activation(psum[6][:, 0:512], _x[:, 512:1024],
                                                AF.Square, accum_out=stc(d_t1)),
            [bslot], [bPS[6], bST[d_t1]])
        vop(lambda e, s: e.tensor_add(stc(d_sq), stc(d_sq), stc(d_t1)),
            [bST[d_sq], bST[d_t1]], [bST[d_sq]])
        vop(lambda e, s: e.tensor_scalar_mul(stc(d_m), stc(d_sum), 1.0 / D),
            [bST[d_sum]], [bST[d_m]])
        vop(lambda e, s: e.tensor_mul(stc(d_t2), stc(d_m), stc(d_m)),
            [bST[d_m]], [bST[d_t2]])
        vop(lambda e, s: e.scalar_tensor_tensor(
            stc(d_var), stc(d_sq), 1.0 / D, stc(d_t2), ALU.mult, ALU.subtract),
            [bST[d_sq], bST[d_t2]], [bST[d_var]])
        sop(lambda e, s: e.activation(stc(d_t1), stc(d_var), AF.Ln,
                                      bias=stc(c_eps)),
            [bST[d_var], bST[c_eps]], [bST[d_t1]])
        sop(lambda e, s: e.activation(stc(d_r), stc(d_t1), AF.Exp,
                                      scale=np.float32(-0.5)),
            [bST[d_t1]], [bST[d_r]])
        vop(lambda e, s: e.scalar_tensor_tensor(
            stc(d_nmr), stc(d_m), -1.0, stc(d_r), ALU.mult, ALU.mult),
            [bST[d_m], bST[d_r]], [bST[d_nmr]])
        sop(lambda e, s, _x=xslot: e.activation(_x, _x, AF.Identity,
                                                bias=stc(d_nmr), scale=stc(d_r)),
            [bST[d_nmr], bST[d_r], bslot], [bslot])

    def s_pre(i):
        L = 128 * (i + 1)
        nkt = (L + 511) // 512
        cnt = min(2, nkt - 1)
        for kt in range(cnt):
            mm(psum[kt][:, 0:512], qT[:, 128 * i:128 * i + 128],
               kT[:, 512 * kt:512 * kt + 512], True, True,
               [bqT, bkT], [bPS[kt]])
        return cnt

    def s_phase(i, pre=0):
        L = 128 * (i + 1)
        nkt = (L + 511) // 512
        xs = A5[:, 1024 * (i % 2):1024 * (i % 2) + 1024]
        dma("sync", xs, x_h[128 * i:128 * i + 128, :], [], [bA5[i % 2]])
        for kt in range(nkt):
            w = min(512, L - 512 * kt)
            sb_ = kt % 2
            if kt >= pre:
                mm(psum[sb_][:, 0:w], qT[:, 128 * i:128 * i + 128],
                   kT[:, 512 * kt:512 * kt + w], True, True,
                   [bqT, bkT], [bPS[sb_]])
            last = (kt == nkt - 1)
            if last:
                if w > 128:
                    vop(lambda e, s, _kt=kt, _sb=sb_, _w=w: e.tensor_copy(
                        Pbuf[:, 512 * _kt:512 * _kt + _w - 128],
                        psum[_sb][:, 0:_w - 128]),
                        [bPS[sb_]], [bA2s[kt]])
                vop(lambda e, s, _kt=kt, _sb=sb_, _w=w: e.tensor_add(
                    Pbuf[:, 512 * _kt + _w - 128:512 * _kt + _w],
                    psum[_sb][:, _w - 128:_w], mask128),
                    [bPS[sb_], bmask], [bA2s[kt]])
            else:
                vop(lambda e, s, _kt=kt, _sb=sb_: e.tensor_copy(
                    Pbuf[:, 512 * _kt:512 * _kt + 512], psum[_sb][:, 0:512]),
                    [bPS[sb_]], [bA2s[kt]])
            if kt == 0:
                vop(lambda e, s, _kt=kt, _w=w: e.reduce_max(
                    stc(c_max), Pbuf[:, 512 * _kt:512 * _kt + _w], AX.X),
                    [bA2s[kt]], [bST[c_max]])
            else:
                vop(lambda e, s, _kt=kt, _w=w: e.reduce_max(
                    stc(c_tm), Pbuf[:, 512 * _kt:512 * _kt + _w], AX.X),
                    [bA2s[kt]], [bST[c_tm]])
                vop(lambda e, s: e.tensor_max(stc(c_max), stc(c_max), stc(c_tm)),
                    [bST[c_max], bST[c_tm]], [bST[c_max]])
        vop(lambda e, s: e.tensor_scalar_mul(stc(c_nm8), stc(c_max), -QSCALE),
            [bST[c_max]], [bST[c_nm8]])
        for kt in range(nkt):
            w = min(512, L - 512 * kt)
            acc = c_es if kt == 0 else c_tm
            sop(lambda e, s, _kt=kt, _w=w, _a=acc: e.activation(
                Pbuf[:, 512 * _kt:512 * _kt + _w],
                Pbuf[:, 512 * _kt:512 * _kt + _w], AF.Exp,
                bias=stc(c_nm8), scale=stc(c_q8), accum_out=stc(_a)),
                [bST[c_nm8], bST[c_q8], bA2s[kt]], [bA2s[kt], bST[acc]])
            if kt > 0:
                vop(lambda e, s: e.tensor_add(stc(c_es), stc(c_es), stc(c_tm)),
                    [bST[c_es], bST[c_tm]], [bST[c_es]])
        vop(lambda e, s: e.reciprocal(stc(c_rs), stc(c_es)),
            [bST[c_es]], [bST[c_rs]])

    def pv_phase(i):
        ob = (4, 5)

        def ptrans(kc):
            tb = 2 + (kc % 2)
            tr(psum[tb][:, 0:128], Pbuf[:, 128 * kc:128 * kc + 128], ident[:],
               [bA2s[kc // 4], bident], [bPS[tb]])
            vop(lambda e, s, _kc=kc, _tb=tb: e.tensor_copy(
                A7[:, 128 * (_kc % 4):128 * (_kc % 4) + 128], psum[_tb][:, 0:128]),
                [bPS[tb]], [bA7[kc % 4]])

        ptrans(0)
        if i >= 1:
            ptrans(1)
        for kc in range(i + 1):
            if kc + 2 <= i:
                ptrans(kc + 2)
            for hh in range(2):
                mm(psum[ob[hh]][:, 0:512],
                   A7[:, 128 * (kc % 4):128 * (kc % 4) + 128],
                   A1[:, 1024 * kc + 512 * hh:1024 * kc + 512 * hh + 512],
                   kc == 0, kc == i,
                   [bA7[kc % 4], bA1[(1024 * kc + 512 * hh) // 512]],
                   [bPS[ob[hh]]])
        xs = A5[:, 1024 * (i % 2):1024 * (i % 2) + 1024]
        for hh in range(2):
            vop(lambda e, s, _hh=hh, _b=ob[hh], _xs=xs:
                e.scalar_tensor_tensor(
                    _xs[:, 512 * _hh:512 * _hh + 512], psum[_b][:, 0:512],
                    stc(c_rs), _xs[:, 512 * _hh:512 * _hh + 512],
                    ALU.mult, ALU.add),
                [bPS[ob[hh]], bST[c_rs], bA5[i % 2]], [bA5[i % 2]])
        dma("gpsimd", xa_h[128 * i:128 * i + 128, :], xs, [bA5[i % 2]],
            [bHBM["xa"]])
        emit_ln2p(xs, bA5[i % 2])
        dma("gpsimd", y2_h[128 * i:128 * i + 128, :], xs, [bA5[i % 2]],
            [by2[i]])

    pre = 0
    for i in range(NT):
        s_phase(i, pre)
        pre = s_pre(i + 1) if i + 1 < NT else 0
        pv_phase(i)

    P.barrier()

    sbpr = AW[0:1, 5120:6144]
    bsbpr = Buf("sbpr")
    bprT = A2[:, 4224:5248]
    dma("sync", sbpr, bass.AP(bpr_h.tensor, 0, [[D, 1], [1, D]]), [], [bsbpr])
    for hh in range(2):
        mm(psum[4 + hh][:, 0:512], sones[:], sbpr[:, 512 * hh:512 * hh + 512],
           True, True, [bsones, bsbpr], [bPS[4 + hh]])
        vop(lambda e, s, _hh=hh: e.tensor_copy(
            bprT[:, 512 * _hh:512 * _hh + 512], psum[4 + _hh][:, 0:512]),
            [bPS[4 + hh]], [bbprT])

    w1s = [A4[:, 0:1024], A4[:, 1024:2048]]
    bw1s = [Buf("w1s0"), Buf("w1s1")]
    hbuf = AW[:, 0:4096]
    bhbuf = Buf("hb")
    NPAN_T = T // 512

    def gen_p3():
        dma("sync", A5[:, 0:1024], y2_h[0:128, :], [by2[0]], [bA5[0]])
        for t in range(NT):
            xs = A5[:, 1024 * (t % 2):1024 * (t % 2) + 1024]
            if t + 1 < NT:
                dma("sync",
                    A5[:, 1024 * ((t + 1) % 2):1024 * ((t + 1) % 2) + 1024],
                    y2_h[128 * (t + 1):128 * (t + 1) + 128, :],
                    [by2[t + 1]], [bA5[(t + 1) % 2]])
            for c in range(ND):
                tb = c % 2
                tr(psum[tb][:, 0:128], xs[:, 128 * c:128 * c + 128], ident[:],
                   [bA5[t % 2], bident], [bPS[tb]])
                vop(lambda e, s, _c=c, _t=t, _tb=tb: e.tensor_copy(
                    A1[:, 4096 * _c + 128 * _t:4096 * _c + 128 * _t + 128],
                    psum[_tb][:, 0:128]),
                    [bPS[tb]], [bA1[(4096 * c + 128 * t) // 512]])
            if t % 4 == 3:
                yield

    def gen_fc():
        for fc in range(NF):
            dma("sync", w1s[fc % 2][:], w1t_h[128 * fc:128 * fc + 128, :],
                [bHBM["w1t"]], [bw1s[fc % 2]])
            for p in range(NPAN_T):
                hb = 2 + (fc * NPAN_T + p) % 2
                for c in range(ND):
                    mm(psum[hb][:, 0:512], w1s[fc % 2][:, 128 * c:128 * c + 128],
                       A1[:, 4096 * c + 512 * p:4096 * c + 512 * p + 512],
                       c == 0, c == ND - 1,
                       [bw1s[fc % 2], bA1[(4096 * c + 512 * p) // 512]],
                       [bPS[hb]])
                sop(lambda e, s, _fc=fc, _p=p, _hb=hb: e.activation(
                    hbuf[:, 512 * _p:512 * _p + 512], psum[_hb][:, 0:512],
                    AF.Gelu, bias=sbfc[:, _fc:_fc + 1]),
                    [bPS[hb], bsbfc], [bhbuf])
                if fc == 0:
                    yield
            dma("gpsimd", hT_h[128 * fc:128 * fc + 128, 0:T],
                hbuf[:, 0:T], [bhbuf], [bhT[fc]])

    g3, gf = gen_p3(), gen_fc()
    a3_, af_ = True, True
    while a3_ or af_:
        if a3_:
            a3_ = next(g3, _SENT) is not _SENT
        if af_:
            af_ = next(gf, _SENT) is not _SENT


    w2s = [A2[:, 1024 * j:1024 * j + 1024] for j in range(4)] + \
          [A4[:, 0:1024], A4[:, 1024:2048]]
    bw2s = [Buf(f"w2s{j}") for j in range(6)]
    hts = [AW[:, 4096 + 256 * j:4096 + 256 * j + 256] for j in range(4)]
    bhts = [Buf(f"hts{j}") for j in range(4)]

    for sp in range(NSUP):
        pb = 4 * ((sp + 1) % 2)
        for tt in range(2):
            t = 2 * sp + tt
            xs = A5[:, 1024 * tt:1024 * tt + 1024]
            dma("gpsimd", xs, xa_h[128 * t:128 * t + 128, :], [], [bA5[tt]],
                extra=[bHBM["xa"].writer])
        for fc in range(NF):
            j = fc % 6
            dma("sync" if fc % 2 == 0 else "scalar",
                w2s[j][:], w2t_h[128 * fc:128 * fc + 128, :],
                [], [bw2s[j]] + ([bw1s[j - 4]] if j >= 4 else []),
                extra=[bHBM["w2t"].writer])
            dma("sync", hts[fc % 4][:],
                hT_h[128 * fc:128 * fc + 128, 256 * sp:256 * sp + 256],
                [bhT[fc]], [bhts[fc % 4]])
            for tt in range(2):
                for hh in range(2):
                    mm(psum[pb + 2 * tt + hh][:, 0:512],
                       hts[fc % 4][:, 128 * tt:128 * tt + 128],
                       w2s[j][:, 512 * hh:512 * hh + 512],
                       fc == 0, fc == NF - 1,
                       [bhts[fc % 4], bw2s[j]], [bPS[pb + 2 * tt + hh]])
        for tt in range(2):
            t = 2 * sp + tt
            xs = A5[:, 1024 * tt:1024 * tt + 1024]
            for hh in range(2):
                bk = pb + 2 * tt + hh
                vop(lambda e, s, _bk=bk, _xs=xs, _hh=hh: e.tensor_add(
                    _xs[:, 512 * _hh:512 * _hh + 512], psum[_bk][:, 0:512],
                    _xs[:, 512 * _hh:512 * _hh + 512]),
                    [bPS[bk], bA5[tt]], [bA5[tt]])
                vop(lambda e, s, _bk=bk, _xs=xs, _hh=hh: e.tensor_add(
                    _xs[:, 512 * _hh:512 * _hh + 512],
                    _xs[:, 512 * _hh:512 * _hh + 512],
                    bprT[:, 512 * _hh:512 * _hh + 512]),
                    [bA5[tt], bbprT], [bA5[tt]])
            dma("gpsimd", y_h[128 * t:128 * t + 128, :], xs, [bA5[tt]],
                [bHBM["y"]])

    final_st = [(s, P.ctr[s]) for s in Prog.DPOOL["gpsimd"] if P.ctr[s] > 0]

    with es:
        sems = {}
        for s in P.ctr:
            sems[s] = es.enter_context(nc.semaphore(s))
        with nc.Block() as block:

            @block.tensor
            def _(eng):
                for run in P.sched["tensor"]:
                    run(eng, sems)

            @block.vector
            def _(eng):
                for run in P.sched["vector"]:
                    run(eng, sems)

            @block.scalar
            def _(eng):
                for run in P.sched["scalar"]:
                    run(eng, sems)

            @block.sync
            def _(eng):
                for run in P.sched["sync"]:
                    run(eng, sems)

            @block.gpsimd
            def _(eng):
                for run in P.sched["gpsimd"]:
                    run(eng, sems)
                for s, v in final_st:
                    eng.wait_ge(sems[s], v)

    return nc


def make_masks():
    col = np.arange(128, dtype=np.int64)[None, :]
    row = np.arange(128, dtype=np.int64)[:, None]
    return np.where(col <= row, np.float32(0.0), np.float32(NEG)).astype(np.float32)


def kernel(L0_attn_X, L0_attn_Wq, L0_attn_Wk, L0_attn_Wv,
           L0_ffn_W_fc, L0_ffn_b_fc, L0_ffn_A_fc, L0_ffn_B_fc,
           L0_ffn_W_proj, L0_ffn_b_proj, L0_ffn_A_proj, L0_ffn_B_proj):
    X = np.ascontiguousarray(np.asarray(L0_attn_X, dtype=np.float32))
    B, T, _ = X.shape
    assert B == 8 and T % 512 == 0

    nc = build_program(T)
    common = {
        "wq": np.ascontiguousarray(L0_attn_Wq, dtype=np.float32),
        "wk": np.ascontiguousarray(L0_attn_Wk, dtype=np.float32),
        "wv": np.ascontiguousarray(L0_attn_Wv, dtype=np.float32),
        "wfc": np.ascontiguousarray(L0_ffn_W_fc, dtype=np.float32),
        "bfc": np.ascontiguousarray(L0_ffn_b_fc, dtype=np.float32),
        "afc": np.ascontiguousarray(L0_ffn_A_fc, dtype=np.float32),
        "bfcB": np.ascontiguousarray(L0_ffn_B_fc, dtype=np.float32),
        "wpr": np.ascontiguousarray(L0_ffn_W_proj, dtype=np.float32),
        "bpr": np.ascontiguousarray(L0_ffn_b_proj, dtype=np.float32),
        "apr": np.ascontiguousarray(L0_ffn_A_proj, dtype=np.float32),
        "bprB": np.ascontiguousarray(L0_ffn_B_proj, dtype=np.float32),
        "ident": np.eye(128, dtype=np.float32),
        "cmask": make_masks(),
    }
    in_maps = [dict(common, x=np.ascontiguousarray(X[i])) for i in range(8)]
    res = run_bass_kernel_spmd(nc, in_maps, core_ids=list(range(8)))
    out = np.stack([np.asarray(r["y"], dtype=np.float32) for r in res.results])
    return out
